# version 06
# baseline (speedup 1.0000x reference)
import contextlib
import numpy as np
import ml_dtypes
import concourse.bass as bass
import concourse.mybir as mybir
from concourse.bass_utils import run_bass_kernel_spmd

F32 = mybir.dt.float32
BF16 = mybir.dt.bfloat16
AF = mybir.ActivationFunctionType
ALU = mybir.AluOpType
AX = mybir.AxisListType

B, S, D = 4, 2048, 2048
H, KVH, HD, HALF = 16, 8, 128, 64
DH = 8192
NTOK = 1024
NEG = np.float32(-1.0e9)

_cache = {}
_DEBUG = False


def _host_tables():
    if "t" in _cache:
        return _cache["t"]
    base32 = np.float32(2.0) ** (-(np.float32(2.0) ** np.float32(-1.0)))
    slopes32 = (np.float64(base32) ** np.arange(1, 17, dtype=np.float64)).astype(np.float32)
    au = np.abs(np.arange(-2047, 2048))
    E0 = (slopes32[:, None] * (-au[None, :]).astype(np.float32)).astype(np.float32)
    E0[:, au <= 256] = NEG
    Epad = np.concatenate([np.zeros((16, 127), np.float32), E0], axis=1)
    idx = np.clip((np.arange(4095)[None, :] - np.arange(128)[:, None]) + 127, 0, Epad.shape[1] - 1)
    T_A = np.ascontiguousarray(Epad[:, idx])
    idx2 = (np.arange(4095)[None, :] - np.arange(128)[:, None]) - 2048 + 127
    ok = (idx2 >= 127) & (idx2 < Epad.shape[1])
    T_B = np.ascontiguousarray(Epad[:, np.clip(idx2, 0, Epad.shape[1] - 1)] *
                               ok[None, :, :].astype(np.float32))

    freqs = 1.0 / (10000.0 ** (np.arange(HALF, dtype=np.float64) / HALF))
    ph = freqs[:, None] * np.arange(S, dtype=np.float64)[None, :]

    def hilo(v):
        hi = v.astype(np.float32)
        lo = (v - hi.astype(np.float64)).astype(np.float32)
        return np.ascontiguousarray(hi), np.ascontiguousarray(lo)
    sin_hi, sin_lo = hilo(np.sin(ph))
    cos_hi, cos_lo = hilo(np.cos(ph))
    _cache["t"] = dict(
        T_A=T_A, T_B=T_B, sin_hi=sin_hi, sin_lo=sin_lo, cos_hi=cos_hi, cos_lo=cos_lo,
        ident16=np.eye(128, dtype=ml_dtypes.bfloat16),
        ident32=np.eye(128, dtype=np.float32),
        ones_row=np.ones((1, 128), np.float32),
        ones_col16=np.ones((128, 1), ml_dtypes.bfloat16),
        sqrt128=np.sqrt(np.float32(128.0)).astype(np.float32).reshape(1, 1),
    )
    return _cache["t"]


class P:
    ENGS = ("sync", "tensor", "vector", "scalar", "gpsimd")

    def __init__(self, nc, es):
        self.nc = nc
        self.q = {e: [] for e in self.ENGS}
        self.sems = {}
        for name in ("qS", "qA", "qG", "qT", "tT", "tV", "tA", "tG", "qO"):
            h = es.enter_context(nc.semaphore(name))
            self.sems[name] = [h, 0]
        self.issuer = {"qS": "sync", "qA": "scalar", "qG": "gpsimd", "qT": "gpsimd", "qO": "sync"}
        self.boundaries = {k: set() for k in self.issuer}

    def n(self, name):
        return self.sems[name][1]

    def emit(self, eng, fn, inc=None, k=1):
        if inc is None:
            self.q[eng].append(fn)
            return None
        s = self.sems[inc]
        s[1] += k
        self.q[eng].append(lambda e, fn=fn, h=s[0], k=k: fn(e).then_inc(h, k))
        return s[1]

    def wait(self, eng, name, val=None):
        s = self.sems[name]
        v = s[1] if val is None else val
        if v <= 0:
            return
        iss = self.issuer.get(name)
        if iss is not None and eng != iss and v not in self.boundaries[name]:
            self.boundaries[name].add(v)
            self.q[iss].append(lambda e, h=s[0], v=v: e.wait_ge(h, v))
        self.q[eng].append(lambda e, h=s[0], v=v: e.wait_ge(h, v))

    def realize(self, block):
        for eng in self.ENGS:
            ops = self.q[eng]

            def runner(engobj, ops=ops):
                for f in ops:
                    f(engobj)
            getattr(block, eng)(runner)


def build_program():
    nc = bass.Bass("TRN2", target_bir_lowering=False, debug=False)
    dt = nc.dram_tensor

    x_own = dt("x_own", [NTOK, D], F32, kind="ExternalInput").ap()
    x_oth = dt("x_oth", [NTOK, D], F32, kind="ExternalInput").ap()
    Wq = dt("Wq", [D, D], F32, kind="ExternalInput").ap()
    Wkv = dt("Wkv", [D, D], F32, kind="ExternalInput").ap()
    Wo = dt("Wo", [D, D], F32, kind="ExternalInput").ap()
    bo = dt("bo", [1, D], F32, kind="ExternalInput").ap()
    wg = dt("wg", [DH, D], F32, kind="ExternalInput").ap()
    wu = dt("wu", [DH, D], F32, kind="ExternalInput").ap()
    wd = dt("wd", [D, DH], F32, kind="ExternalInput").ap()
    T_A = dt("T_A", [16, 128, 4095], F32, kind="ExternalInput").ap()
    T_X = dt("T_X", [16, 128, 4095], F32, kind="ExternalInput").ap()
    tabs_in = {}
    for nm in ("sinq_hi", "sinq_lo", "cosq_hi", "cosq_lo"):
        tabs_in[nm] = dt(nm, [HALF, NTOK], F32, kind="ExternalInput").ap()
    for nm in ("sink_hi", "sink_lo", "cosk_hi", "cosk_lo"):
        tabs_in[nm] = dt(nm, [HALF, S], F32, kind="ExternalInput").ap()
    identin = dt("identin", [128, 128], BF16, kind="ExternalInput").ap()
    ident32in = dt("ident32in", [128, 128], F32, kind="ExternalInput").ap()
    onesrow = dt("onesrow", [1, 128], F32, kind="ExternalInput").ap()
    onescol = dt("onescol", [128, 1], BF16, kind="ExternalInput").ap()
    sq128 = dt("sq128", [1, 1], F32, kind="ExternalInput").ap()
    expmask = dt("expmask", [128, 8], F32, kind="ExternalInput").ap()
    keepcol = dt("keepcol", [128, 8], F32, kind="ExternalInput").ap()
    keepcol2 = dt("keepcol2", [128, 1], F32, kind="ExternalInput").ap()
    y = dt("y", [NTOK, D], F32, kind="ExternalOutput").ap()
    wgs = dt("wgs", [16 * 16 * 512, 128], BF16, kind="Internal").ap()
    wus = dt("wus", [16 * 16 * 512, 128], BF16, kind="Internal").ap()
    wds = dt("wds", [D, DH], BF16, kind="Internal").ap()
    wdT_d = dt("wdT_d", [DH, D], BF16, kind="Internal").ap()
    x2d = dt("x2d", [NTOK, D], F32, kind="Internal").ap()
    if _DEBUG:
        dbg_x2 = dt("dbg_x2", [NTOK, D], F32, kind="ExternalOutput").ap()
        dbg_xnf = dt("dbg_xnf", [128, 16 * 512], F32, kind="ExternalOutput").ap()
        dbg_xt = dt("dbg_xt", [128, D], F32, kind="ExternalOutput").ap()
        dbg_stat = dt("dbg_stat", [128, 24], F32, kind="ExternalOutput").ap()
        dbg_st4 = dt("dbg_st4", [128, 96], F32, kind="ExternalOutput").ap()
        dbg_xn = dt("dbg_xn", [128, D], F32, kind="ExternalOutput").ap()
        dbg_kps = dt("dbg_kps", [128, 512], F32, kind="ExternalOutput").ap()
        dbg_tab = dt("dbg_tab", [64, 4 * S], F32, kind="ExternalOutput").ap()
        dbg_wgT = dt("dbg_wgT", [128, 16 * 512], F32, kind="ExternalOutput").ap()
        dbg_xnf2 = dt("dbg_xnf2", [128, 16 * 512], F32, kind="ExternalOutput").ap()
        dbg_gated = dt("dbg_gated", [128, 8 * 512], F32, kind="ExternalOutput").ap()
        dbg_ffo = dt("dbg_ffo", [128, 4 * D], F32, kind="ExternalOutput").ap()
        dbg_k = dt("dbg_k", [128, KVH * S], F32, kind="ExternalOutput").ap()
        dbg_q = dt("dbg_q", [128, H * NTOK], F32, kind="ExternalOutput").ap()
        dbg_v = dt("dbg_v", [128, 16 * 1024], F32, kind="ExternalOutput").ap()
        dbg_af = dt("dbg_af", [128, H * 512], F32, kind="ExternalOutput").ap()

    root = contextlib.ExitStack()
    with root:
        def SB(st, name, shape, dty):
            return st.enter_context(nc.sbuf_tensor(name, shape, dty))

        def PS(st, name, shape, dty=F32):
            return st.enter_context(nc.psum_tensor(name, shape, dty))

        p = P(nc, root)
        qS, qA, qG, tT, tV, tA, tG, qO = "qS", "qA", "qG", "tT", "tV", "tA", "tG", "qO"

        ident = SB(root, "ident", [128, 128], BF16)
        ident32 = SB(root, "ident32", [128, 128], F32)
        ones_r = SB(root, "ones_r", [1, 128], F32)
        ones_c = SB(root, "ones_c", [128, 1], BF16)
        rmask = SB(root, "rmask", [128, 8], F32)
        kcol = SB(root, "kcol", [128, 8], F32)
        kcol2 = SB(root, "kcol2", [128, 1], F32)
        eps_t = SB(root, "eps_t", [128, 1], F32)
        sc_rec = SB(root, "sc_rec", [1, 1], F32)
        stat = SB(root, "stat", [128, 24], F32)
        attn_st = contextlib.ExitStack()
        K_sb = SB(attn_st, "K_sb", [128, KVH * S], BF16)
        V_sb = SB(attn_st, "V_sb", [128, 16 * 1024], BF16)
        q_sb = SB(attn_st, "q_sb", [128, H * NTOK], BF16)
        cosC = SB(attn_st, "cosC", [HALF, NTOK], BF16)
        sinC = SB(attn_st, "sinC", [HALF, NTOK], BF16)

        cell = {}

        def phase_barrier():
            for eng in ("sync", "gpsimd", "scalar", "vector", "tensor"):
                for sem in (tT, tV, tA, tG):
                    p.wait(eng, sem)

        def load_wcols(w_ap, c0, ncols):
            wt = cell["wtile"]
            src = w_ap[:, c0:c0 + ncols].rearrange("(dc q) c -> q dc c", q=128)
            dstv = wt[:, 0:16 * ncols].rearrange("q (dc c) -> q dc c", c=ncols)
            p.wait("gpsimd", tT, p.n(tT))
            p.emit("gpsimd", lambda e, a=dstv, b=src: e.dma_start(out=a, in_=b), inc=qG, k=16)
            p.wait("tensor", qG)

        def proj_16(lhs_fn, rhs_fn, pst, inc_last=True):
            if not isinstance(pst, bass.AP):
                pst = pst[:, :]
            for dc in range(16):
                w = lhs_fn(dc)
                p.emit("tensor", lambda e, w=w: e.ldweights(w))
                p.emit("tensor", lambda e, o=pst, w=w, r=rhs_fn(dc), s=(dc == 0), t=(dc == 15):
                       e.matmul(o, w, r, start=s, stop=t),
                       inc=(tT if (dc == 15 and inc_last) else None))

        def transpose_128(src_fn, n_t, dst_fn, idmat):
            if not isinstance(idmat, bass.AP):
                idmat = idmat[:, :]
            p.wait("tensor", tV)
            ptr, ptr2 = cell["ps_tr"], cell["ps_tr2"]
            done = 0
            while done < n_t:
                grp = min(8, n_t - done)
                pst = ptr if (done // 8) % 2 == 0 else ptr2
                for i in range(grp):
                    p.emit("tensor", lambda e, s=src_fn(done + i), d=pst[:, i * 128:(i + 1) * 128]:
                           e.transpose(d, s, idmat), inc=(tT if i == grp - 1 else None))
                p.wait("vector", tT)
                for i in range(grp):
                    p.emit("vector", lambda e, s=pst[:, i * 128:(i + 1) * 128], d=dst_fn(done + i):
                           e.tensor_copy(out=d, in_=s), inc=(tV if i == grp - 1 else None))
                p.wait("tensor", tV)
                done += grp

        ln_state = {"last_sq_red": 0, "last_norm": {}}

        def ln_stats_norm(src_tile, sqbuf, out_fn):
            p.wait("vector", tV)
            p.emit("vector", lambda e, a=src_tile: e.reduce_sum(stat[:, 0:1], a, axis=AX.X), inc=tV)
            p.wait("scalar", tV)
            p.emit("scalar", lambda e, a=src_tile, q=sqbuf: e.activation(q, a, AF.Square), inc=tA)
            p.wait("vector", tA)
            p.emit("vector", lambda e, q=sqbuf: e.reduce_sum(stat[:, 16:17], q, axis=AX.X), inc=tV)
            ln_state["last_sq_red"] = p.n(tV)
            p.wait("vector", tV)
            p.emit("vector", lambda e: e.tensor_scalar(stat[:, 8:9], stat[:, 0:1], 1.0 / D, None, ALU.mult), inc=tV)
            p.wait("vector", tV)
            p.emit("vector", lambda e: e.tensor_mul(stat[:, 12:13], stat[:, 8:9], stat[:, 8:9]), inc=tV)
            p.wait("vector", tV)
            p.emit("vector", lambda e: e.scalar_tensor_tensor(stat[:, 16:17], stat[:, 16:17], 1.0 / D,
                                                              stat[:, 12:13], ALU.mult, ALU.subtract), inc=tV)
            p.wait("scalar", tV)
            p.emit("scalar", lambda e: e.activation(stat[:, 20:21], stat[:, 16:17], AF.Ln, bias=eps_t[:]), inc=tA)
            p.wait("scalar", tA)
            p.emit("scalar", lambda e: e.activation(stat[:, 20:21], stat[:, 20:21], AF.Exp, scale=-0.5), inc=tA)
            p.wait("vector", tA)
            p.wait("vector", tV)
            out_fn()

        ph1 = contextlib.ExitStack()
        cosk = SB(ph1, "cosk", [HALF, S], BF16)
        sink = SB(ph1, "sink", [HALF, S], BF16)
        cosq = SB(ph1, "cosq", [HALF, NTOK], BF16)
        sinq = SB(ph1, "sinq", [HALF, NTOK], BF16)
        c_row = SB(ph1, "c_row", [1, NTOK], F32)
        x_t = [SB(ph1, "x_t0", [128, D], F32), SB(ph1, "x_t1", [128, D], F32)]
        xn1 = SB(ph1, "xn1", [128, D], BF16)
        xn1F = SB(ph1, "xn1F", [128, 16 * 512], BF16)
        cell["wtile"] = SB(ph1, "wtile1", [128, 16 * 512], BF16)
        sq_s = SB(ph1, "sq_s", [128, D], F32)
        rt = [SB(ph1, f"rt{i}", [HALF, 512], F32) for i in range(4)]
        csb = SB(ph1, "csb", [HALF, 512], F32)
        c_col = SB(ph1, "c_col", [128, 4], F32)
        tab_st = contextlib.ExitStack()
        hi_sb = SB(tab_st, "hi_sb", [HALF, S], F32)
        lo_sb = SB(tab_st, "lo_sb", [HALF, S], F32)
        ps_main = PS(ph1, "ps1_main", [128, S], F32)
        cell["ps_tr"] = PS(ph1, "ps1_tr", [128, 1024], BF16)
        cell["ps_tr2"] = PS(ph1, "ps1_tr2", [128, 1024], BF16)
        ps_aux = PS(ph1, "ps1_aux", [128, 512], F32)
        ps_p = [ps_main[:, 0:512], ps_main[:, 512:1024]]

        wgv = wg.rearrange("(hc hl) (dc q) -> hc dc hl q", hl=512, q=128)
        wuv = wu.rearrange("(hc hl) (dc q) -> hc dc hl q", hl=512, q=128)
        wgsv = wgs.rearrange("(hc dc hl) q -> hc dc hl q", dc=16, hl=512)
        wusv = wus.rearrange("(hc dc hl) q -> hc dc hl q", dc=16, hl=512)
        qT = "qT"
        for hc in range(16):
            p.emit("gpsimd", lambda e, a=wgsv[hc], b=wgv[hc]:
                   e.dma_start(out=a, in_=b), inc=qT, k=16)
            p.emit("gpsimd", lambda e, a=wusv[hc], b=wuv[hc]:
                   e.dma_start(out=a, in_=b), inc=qT, k=16)
        for dc in range(8):
            r0 = dc * 256
            p.emit("gpsimd", lambda e, a=wds[r0:r0 + 256, :], b=wd[r0:r0 + 256, :]:
                   e.dma_start(out=a, in_=b), inc=qT, k=16)
        stash_done = p.n(qT)

        for dst, hin, lin, n in ((cosk, "cosk_hi", "cosk_lo", S), (sink, "sink_hi", "sink_lo", S),
                                 (cosq, "cosq_hi", "cosq_lo", NTOK), (sinq, "sinq_hi", "sinq_lo", NTOK)):
            p.emit("sync", lambda e, a=hi_sb[:, 0:n], b=tabs_in[hin]: e.dma_start(out=a, in_=b), inc=qS, k=16)
            p.emit("sync", lambda e, a=lo_sb[:, 0:n], b=tabs_in[lin]: e.dma_start(out=a, in_=b), inc=qS, k=16)
            p.wait("vector", qS)
            p.emit("vector", lambda e, d=dst[:, 0:n], a=hi_sb[:, 0:n], b=lo_sb[:, 0:n]:
                   e.tensor_add(d, a, b), inc=tV)
            p.wait("sync", tV)
        for dsb, din in ((ident, identin), (ident32, ident32in), (ones_r, onesrow),
                         (ones_c, onescol), (sc_rec, sq128), (rmask, expmask),
                         (kcol, keepcol), (kcol2, keepcol2)):
            p.emit("sync", lambda e, a=dsb[:, :], b=din: e.dma_start(out=a, in_=b), inc=qS, k=16)
        p.wait("vector", qS)
        p.emit("vector", lambda e: e.memset(eps_t[:], 1e-5), inc=tV)
        p.emit("vector", lambda e: e.reciprocal(sc_rec[:], sc_rec[:]), inc=tV)
        p.wait("tensor", qS)
        p.wait("scalar", qS)
        p.wait("gpsimd", qS)
        p.wait("sync", tV)
        tab_st.close()
        wdstg = [SB(ph1, "wdstg0", [128, D], BF16), SB(ph1, "wdstg1", [128, D], BF16)]
        wd_tr_n = {0: 0, 1: 0}
        p.wait("sync", qT, stash_done)
        for hch in range(64):
            par = hch % 2
            p.wait("sync", qS, wd_tr_n[par])
            p.emit("sync", lambda e, d=wdstg[par][:, :], s=wds[:, hch * 128:(hch + 1) * 128]:
                   e.dma_start(out=d, in_=s, transpose=True), inc=qS, k=16)
            p.wait("sync", qS)
            p.emit("sync", lambda e, d=wdT_d[hch * 128:(hch + 1) * 128, :], s=wdstg[par][:, :]:
                   e.dma_start(out=d, in_=s), inc=qS, k=16)
            wd_tr_n[par] = p.n(qS)
        wd_tr_done = p.n(qS)

        for sp in range(4):
            src = x_own if sp < 2 else x_oth
            base = (sp % 2) * 512
            for tt in range(4):
                par = tt % 2
                xt_ = x_t[par]
                p.wait("scalar", tV, ln_state["last_norm"].get(par, 0))
                p.emit("scalar", lambda e, a=xt_[:, :], b=src[base + tt * 128: base + (tt + 1) * 128, :]:
                       e.dma_start(out=a, in_=b), inc=qA, k=16)
                p.wait("vector", qA)
                p.wait("scalar", qA)

                def norm_out(xt_=xt_, sp=sp, tt=tt):
                    p.emit("vector", lambda e, a=xt_[:, :]:
                           e.tensor_scalar(xn1[:, :], a, stat[:, 8:9], stat[:, 20:21],
                                           ALU.subtract, ALU.mult), inc=tV)
                    ln_state["last_norm"][tt % 2] = p.n(tV)
                    if sp < 2:
                        p.wait("vector", tV)
                        p.emit("vector", lambda e, cc=c_col[:, tt:tt + 1]:
                               e.reduce_sum(cc, xn1[:, :], axis=AX.X, apply_absolute_value=True), inc=tV)
                ln_stats_norm(xt_[:, :], sq_s[:, :], norm_out)
                if _DEBUG and sp == 0:
                    p.emit("vector", lambda e, d_=dbg_st4, tt=tt:
                           e.tensor_copy(out=sq_s[:, 1024 + tt * 24:1024 + (tt + 1) * 24], in_=stat[:, :]), inc=tV)
                    if tt == 0:
                        p.emit("vector", lambda e: e.tensor_copy(out=sq_s[:, 256:256 + D - 256],
                                                                 in_=xn1[:, 0:D - 256]), inc=tV)
                        p.wait("sync", tV)
                        p.emit("sync", lambda e: e.dma_start(out=dbg_xn[:, 0:D - 256],
                                                             in_=sq_s[:, 256:256 + D - 256]), inc=qS, k=16)
                        p.wait("vector", qS)
                if _DEBUG and sp == 0 and tt == 3:
                    p.wait("sync", tV)
                    p.emit("sync", lambda e: e.dma_start(out=dbg_st4[:, :], in_=sq_s[:, 1024:1024 + 96]), inc=qS, k=16)
                    p.wait("vector", qS)
                if _DEBUG and sp == 0 and tt == 0:
                    p.wait("sync", tV)
                    p.emit("sync", lambda e, d=dbg_xt[:, :], s=xt_[:, :]: e.dma_start(out=d, in_=s), inc=qS, k=16)
                    p.emit("sync", lambda e: e.dma_start(out=dbg_stat[:, :], in_=stat[:, :]), inc=qS, k=16)
                    p.wait("vector", qS)
                transpose_128(lambda i: xn1[:, i * 128:(i + 1) * 128], 16,
                              lambda i, tt=tt: xn1F[:, i * 512 + tt * 128: i * 512 + tt * 128 + 128], ident)
            xn1F_ready = p.n(tV)
            if _DEBUG and sp == 0:
                for cc in range(4):
                    p.emit("vector", lambda e, d=sq_s[:, :], s=xn1F[:, cc * 2048:(cc + 1) * 2048]:
                           e.tensor_copy(out=d, in_=s), inc=tV)
                    p.wait("sync", tV)
                    p.emit("sync", lambda e, d=dbg_xnf[:, cc * 2048:(cc + 1) * 2048], s=sq_s[:, :]:
                           e.dma_start(out=d, in_=s), inc=qS, k=16)
                    p.wait("vector", qS)

            if sp < 2:
                p.wait("scalar", tV, xn1F_ready)
                p.emit("scalar", lambda e: e.activation(c_col[:, 0:4], c_col[:, 0:4], AF.Sigmoid), inc=tA)
                p.wait("tensor", tA)
                for tt in range(4):
                    p.emit("tensor", lambda e, s=c_col[:, tt:tt + 1]:
                           e.transpose(ps_aux[0:1, 0:128], s, ident32[:, :]), inc=tT)
                    p.wait("vector", tT)
                    p.emit("vector", lambda e, d=c_row[0:1, sp * 512 + tt * 128: sp * 512 + (tt + 1) * 128]:
                           e.tensor_copy(out=d, in_=ps_aux[0:1, 0:128]), inc=tV)
                    p.wait("tensor", tV)
                p.wait("vector", tV)
                p.emit("vector", lambda e, a=c_row[0:1, sp * 512:(sp + 1) * 512]:
                       e.tensor_scalar(a, a, sc_rec[0:1, 0:1], None, ALU.mult), inc=tV)
                p.wait("tensor", tV)
                p.emit("tensor", lambda e, r=c_row[0:1, sp * 512:(sp + 1) * 512]:
                       e.matmul(ps_aux[0:64, :], ones_r[0:1, 0:64], r, start=True, stop=True), inc=tT)
                p.wait("vector", tT)
                p.emit("vector", lambda e: e.tensor_copy(out=csb[:, :], in_=ps_aux[0:64, :]), inc=tV)
                p.wait("tensor", tV)
                p.wait("vector", tV)
                p.emit("vector", lambda e, d=cosC[:, sp * 512:(sp + 1) * 512],
                       t=cosq[:, sp * 512:(sp + 1) * 512]: e.tensor_mul(d, t, csb[:, :]), inc=tV)
                p.emit("vector", lambda e, d=sinC[:, sp * 512:(sp + 1) * 512],
                       t=sinq[:, sp * 512:(sp + 1) * 512]: e.tensor_mul(d, t, csb[:, :]), inc=tV)

            def rope_evac(pst, tabc, tabs, dst_lo, dst_hi):
                p.wait("vector", tV)
                p.emit("vector", lambda e, a=pst[0:64, :], c=tabc: e.tensor_mul(rt[0][:, :], a, c), inc=tV)
                p.emit("vector", lambda e, a=pst[64:128, :], s2=tabs: e.tensor_mul(rt[1][:, :], a, s2), inc=tV)
                p.emit("vector", lambda e, a=pst[0:64, :], s2=tabs: e.tensor_mul(rt[2][:, :], a, s2), inc=tV)
                p.emit("vector", lambda e, a=pst[64:128, :], c=tabc: e.tensor_mul(rt[3][:, :], a, c), inc=tV)
                p.wait("vector", tV)
                p.emit("vector", lambda e, d=dst_lo: e.tensor_sub(d, rt[0][:, :], rt[1][:, :]), inc=tV)
                p.emit("vector", lambda e, d=dst_hi: e.tensor_add(d, rt[2][:, :], rt[3][:, :]), inc=tV)

            for ocq in range(4):
                load_wcols(Wkv, ocq * 256, 256)
                p.wait("tensor", tV, xn1F_ready)
                for oc2 in range(2):
                    oc = ocq * 2 + oc2
                    wt = cell["wtile"]
                    proj_16(lambda dc, oc2=oc2, wt=wt: wt[:, dc * 256 + oc2 * 128: dc * 256 + oc2 * 128 + 128],
                            lambda dc: xn1F[:, dc * 512:(dc + 1) * 512], ps_p[oc2])
                    p.wait("vector", tT)
                    if _DEBUG and sp == 0 and ocq == 0 and oc2 == 0:
                        p.emit("vector", lambda e: e.tensor_copy(out=sq_s[:, 0:512], in_=ps_p[0]), inc=tV)
                        p.wait("sync", tV)
                        p.emit("sync", lambda e: e.dma_start(out=dbg_kps[:, :], in_=sq_s[:, 0:512]), inc=qS, k=16)
                        p.wait("vector", qS)
                    rope_evac(ps_p[oc2], cosk[:, sp * 512:(sp + 1) * 512], sink[:, sp * 512:(sp + 1) * 512],
                              K_sb[0:64, oc * S + sp * 512: oc * S + (sp + 1) * 512],
                              K_sb[64:128, oc * S + sp * 512: oc * S + (sp + 1) * 512])
                p.wait("tensor", tV)

            for vq in range(2):
                load_wcols(Wkv, 1024 + vq * 512, 512)
                for tt in range(4):
                    if tt >= 2:
                        p.wait("tensor", tV)
                    wt = cell["wtile"]
                    proj_16(lambda dc, tt=tt: xn1F[:, dc * 512 + tt * 128: dc * 512 + tt * 128 + 128],
                            lambda dc, wt=wt: wt[:, dc * 512:(dc + 1) * 512], ps_p[tt % 2])
                    p.wait("vector", tT)
                    kt = sp * 4 + tt
                    p.emit("vector", lambda e, s=ps_p[tt % 2],
                           d=V_sb[:, kt * 1024 + vq * 512: kt * 1024 + (vq + 1) * 512]:
                           e.tensor_copy(out=d, in_=s), inc=tV)
                p.wait("tensor", tV)

            if sp < 2:
                for ocq in range(4):
                    load_wcols(Wq, ocq * 512, 512)
                    for oc2 in range(4):
                        oc = ocq * 4 + oc2
                        if oc2 >= 2:
                            p.wait("tensor", tV)
                        wt = cell["wtile"]
                        proj_16(lambda dc, oc2=oc2, wt=wt:
                                wt[:, dc * 512 + oc2 * 128: dc * 512 + oc2 * 128 + 128],
                                lambda dc: xn1F[:, dc * 512:(dc + 1) * 512], ps_p[oc2 % 2])
                        p.wait("vector", tT)
                        rope_evac(ps_p[oc2 % 2], cosC[:, sp * 512:(sp + 1) * 512],
                                  sinC[:, sp * 512:(sp + 1) * 512],
                                  q_sb[0:64, oc * NTOK + sp * 512: oc * NTOK + (sp + 1) * 512],
                                  q_sb[64:128, oc * NTOK + sp * 512: oc * NTOK + (sp + 1) * 512])
                    p.wait("tensor", tV)

        if _DEBUG:
            for ii, tb in enumerate((cosk, sink, cosq, sinq)):
                n2 = S if ii < 2 else NTOK
                p.emit("vector", lambda e, tb=tb, n2=n2: e.tensor_copy(out=hi_sb[:, 0:n2], in_=tb[:, 0:n2]), inc=tV)
                p.wait("sync", tV)
                p.emit("sync", lambda e, ii=ii, n2=n2: e.dma_start(out=dbg_tab[:, ii * S: ii * S + n2],
                                                                   in_=hi_sb[:, 0:n2]), inc=qS, k=16)
                p.wait("vector", qS)
        phase_barrier()
        ph1.close()

        ph2 = contextlib.ExitStack()
        s_sb = [SB(ph2, "s_sb0", [128, S], F32), SB(ph2, "s_sb1", [128, S], F32)]
        p_sb = [SB(ph2, "p_sb0", [128, S], BF16), SB(ph2, "p_sb1", [128, S], BF16)]
        a_single = SB(ph2, "a_t0", [128, S], F32)
        a_t = [a_single, a_single]
        pT_sb = SB(ph2, "pT_sb", [128, 16 * 512], BF16)
        attnF = SB(ph2, "attnF", [128, H * 512], BF16)
        rinvb = SB(ph2, "rinvb", [128, 512], F32)
        rsum = rinvb[0:1, :]
        nmax = SB(ph2, "nmax", [128, 8], F32)
        ebias = SB(ph2, "ebias", [128, 8], F32)
        cell["wtile"] = SB(ph2, "wtile2", [128, 16 * 256], BF16)
        aout = SB(ph2, "aout", [128, 2 * D], F32)
        x2t = SB(ph2, "x2t", [128, D], F32)
        bo_b = SB(ph2, "bo_b", [128, D], F32)
        sq2 = s_sb[0][:, :]
        ps_main = PS(ph2, "ps2_main", [128, S], F32)
        cell["ps_tr"] = PS(ph2, "ps2_tr", [128, 1024], BF16)
        cell["ps_tr2"] = PS(ph2, "ps2_tr2", [128, 1024], BF16)
        ps_pv = PS(ph2, "ps2_pv", [128, 512], F32)
        ps_aux = PS(ph2, "ps2_aux", [128, 512], F32)
        ps_p = [ps_main[:, 0:512], ps_main[:, 512:1024]]

        bo_sb = x2t[0:1, :]
        p.emit("sync", lambda e: e.dma_start(out=bo_sb, in_=bo), inc=qS, k=16)
        p.wait("tensor", qS)
        for oc in range(4):
            p.emit("tensor", lambda e, r=bo_sb[0:1, oc * 512:(oc + 1) * 512]:
                   e.matmul(ps_aux[:, :], ones_r[0:1, :], r, start=True, stop=True), inc=tT)
            p.wait("vector", tT)
            p.emit("vector", lambda e, d=bo_b[:, oc * 512:(oc + 1) * 512]:
                   e.tensor_copy(out=d, in_=ps_aux[:, :]), inc=tV)
            p.wait("tensor", tV)

        add_n = {0: 0, 1: 0}
        exp_n = {0: 0, 1: 0}
        trans_n = {0: 0, 1: 0}
        for qh in range(2):
            for h in range(H):
                jkv = h // 2
                for qt in range(4):
                    gqt = qh * 4 + qt
                    off = 2047 - gqt * 128
                    r = (qh * H + h) * 4 + qt
                    par = r % 2
                    at_, ss_, pp_ = a_t[par], s_sb[par], p_sb[par]
                    p.wait("sync", tV, max(add_n[0], add_n[1]))
                    p.emit("sync", lambda e, d=at_[:, 0:1024], s=T_A[h, :, off:off + 1024]:
                           e.dma_start(out=d, in_=s), inc=qS, k=16)
                    p.emit("sync", lambda e, d=at_[:, 1024:2048], s=T_X[h, :, off + 1024:off + 2048]:
                           e.dma_start(out=d, in_=s), inc=qS, k=16)
                    qtile = q_sb[:, h * NTOK + gqt * 128: h * NTOK + gqt * 128 + 128]
                    p.emit("tensor", lambda e, w=qtile: e.ldweights(w))
                    for kc in range(4):
                        p.emit("tensor", lambda e, o=ps_main[:, kc * 512:(kc + 1) * 512], w=qtile,
                               r2=K_sb[:, jkv * S + kc * 512: jkv * S + (kc + 1) * 512]:
                               e.matmul(o, w, r2, start=True, stop=True), inc=(tT if kc == 3 else None))
                    p.wait("vector", tT)
                    p.wait("vector", qS)
                    p.wait("vector", tA, exp_n[par])
                    p.emit("vector", lambda e, d=ss_[:, :], a=at_[:, :]:
                           e.tensor_add(d, ps_main[:, :], a), inc=tV)
                    add_n[par] = p.n(tV)
                    p.wait("vector", tV)
                    p.emit("vector", lambda e, s=ss_[:, :], d=nmax[:, par * 4:par * 4 + 1]:
                           e.reduce_max(d, s, axis=AX.X, negate=True), inc=tV)
                    p.wait("vector", tV)
                    p.emit("vector", lambda e, rm=rmask[:, gqt:gqt + 1], d=ebias[:, par * 4:par * 4 + 1],
                           s=nmax[:, par * 4:par * 4 + 1]: e.tensor_mul(d, s, rm), inc=tV)
                    p.wait("scalar", tV)
                    p.wait("scalar", tT, trans_n[par])
                    p.emit("scalar", lambda e, d=pp_[:, :], s=ss_[:, :], sc=rmask[:, gqt:gqt + 1],
                           b=ebias[:, par * 4:par * 4 + 1]:
                           e.activation(d, s, AF.Exp, bias=b, scale=sc), inc=tA)
                    exp_n[par] = p.n(tA)
                    p.wait("gpsimd", tA)
                    p.emit("gpsimd", lambda e, d=pp_, kk=kcol[:, gqt:gqt + 1]:
                           e.tensor_scalar(d[:, 0:64], d[:, 0:64], kk, None, ALU.mult), inc=tG)
                    p.emit("gpsimd", lambda e, d=pp_: e.tensor_scalar(d[:, 1024:1088], d[:, 1024:1088],
                                                                     kcol2[:, 0:1], None, ALU.mult), inc=tG)
                    p.wait("tensor", tG)
                    p.wait("tensor", tV, add_n[par])
                    for halfg in range(2):
                        pst = cell["ps_tr"] if halfg == 0 else cell["ps_tr2"]
                        for i in range(8):
                            kt = halfg * 8 + i
                            p.emit("tensor", lambda e, s=pp_[:, kt * 128:(kt + 1) * 128],
                                   d=pst[:, i * 128:(i + 1) * 128]:
                                   e.transpose(d, s, ident[:, :]), inc=(tT if i == 7 else None))
                        p.wait("vector", tT)
                        for i in range(8):
                            kt = halfg * 8 + i
                            p.emit("vector", lambda e, s=pst[:, i * 128:(i + 1) * 128],
                                   d=pT_sb[:, kt * 512 + qt * 128: kt * 512 + qt * 128 + 128]:
                                   e.tensor_copy(out=d, in_=s), inc=(tV if i == 7 else None))
                        p.wait("tensor", tV)
                    trans_n[par] = p.n(tT)
                p.emit("tensor", lambda e: e.ldweights(ones_c[:, :]))
                for kt in range(16):
                    p.emit("tensor", lambda e, r2=pT_sb[:, kt * 512:(kt + 1) * 512], s=(kt == 0), t=(kt == 15):
                           e.matmul(ps_aux[0:1, :], ones_c[:, :], r2, start=s, stop=t),
                           inc=(tT if kt == 15 else None))
                p.wait("vector", tT)
                p.emit("vector", lambda e: e.tensor_copy(out=rsum[:, :], in_=ps_aux[0:1, :]), inc=tV)
                p.wait("vector", tV)
                p.emit("vector", lambda e: e.reciprocal(rsum[:, :], rsum[:, :]), inc=tV)
                p.wait("tensor", tV)
                p.emit("tensor", lambda e: e.matmul(ps_aux[:, :], ones_r[0:1, :], rsum[0:1, :],
                                                    start=True, stop=True), inc=tT)
                p.wait("vector", tT)
                p.emit("vector", lambda e: e.tensor_copy(out=rinvb[:, :], in_=ps_aux[:, :]), inc=tV)
                p.wait("tensor", tV)
                p.wait("vector", tV)
                for kt in range(16):
                    vt = V_sb[:, kt * 1024 + jkv * 128: kt * 1024 + jkv * 128 + 128]
                    p.emit("tensor", lambda e, w=vt: e.ldweights(w))
                    p.emit("tensor", lambda e, w=vt, r2=pT_sb[:, kt * 512:(kt + 1) * 512],
                           s=(kt == 0), t=(kt == 15):
                           e.matmul(ps_pv[:, :], w, r2, start=s, stop=t), inc=(tT if kt == 15 else None))
                p.wait("vector", tT)
                p.emit("vector", lambda e, d=attnF[:, h * 512:(h + 1) * 512]:
                       e.tensor_mul(d, ps_pv[:, :], rinvb[:, :]), inc=tV)
                p.wait("tensor", tV)

            for pair in range(2):
                for oc in range(8):
                    load_wcols(Wo, oc * 256, 256)
                    for t2 in range(2):
                        tt = pair * 2 + t2
                        wt = cell["wtile"]
                        proj_16(lambda dc, tt=tt: attnF[:, dc * 512 + tt * 128: dc * 512 + tt * 128 + 128],
                                lambda dc, wt=wt: wt[:, dc * 256:(dc + 1) * 256], ps_p[t2][:, 0:256])
                        p.wait("vector", tT)
                        p.emit("vector", lambda e, s=ps_p[t2][:, 0:256],
                               d=aout[:, t2 * D + oc * 256: t2 * D + (oc + 1) * 256],
                               b=bo_b[:, oc * 256:(oc + 1) * 256]: e.tensor_add(d, s, b), inc=tV)
                        p.wait("tensor", tV)
                for t2 in range(2):
                    tt = pair * 2 + t2
                    row0 = qh * 512 + tt * 128
                    av = aout[:, t2 * D:(t2 + 1) * D]
                    p.wait("scalar", qS)
                    p.emit("scalar", lambda e, d=x2t[:, :], s=x_own[row0:row0 + 128, :]:
                           e.dma_start(out=d, in_=s), inc=qA, k=16)

                    def fin2(av=av, row0=row0):
                        p.emit("vector", lambda e, a=av: e.tensor_scalar(a, a, stat[:, 8:9], stat[:, 20:21],
                                                                         ALU.subtract, ALU.mult), inc=tV)
                        p.wait("vector", tV)
                        p.wait("vector", qA)
                        p.emit("vector", lambda e, a=av: e.tensor_add(x2t[:, :], x2t[:, :], a), inc=tV)
                        p.wait("sync", tV)
                        p.emit("sync", lambda e, d=x2d[row0:row0 + 128, :], s=x2t[:, :]:
                               e.dma_start(out=d, in_=s), inc=qS, k=16)
                    ln_stats_norm(av, sq2, fin2)

        if _DEBUG:
            dstg = aout[:, 0:4096]
            for nm_src, nm_dst, ncols in ((K_sb, dbg_k, KVH * S), (q_sb, dbg_q, H * NTOK),
                                          (V_sb, dbg_v, 16 * 1024), (attnF, dbg_af, H * 512)):
                for c0 in range(0, ncols, 4096):
                    w = min(4096, ncols - c0)
                    p.wait("vector", qS)
                    p.emit("vector", lambda e, d=dstg[:, 0:w], s=nm_src[:, c0:c0 + w]:
                           e.tensor_copy(out=d, in_=s), inc=tV)
                    p.wait("sync", tV)
                    p.emit("sync", lambda e, d=nm_dst[:, c0:c0 + w], s=dstg[:, 0:w]:
                           e.dma_start(out=d, in_=s), inc=qS, k=16)
            p.emit("sync", lambda e: e.dma_start(out=dbg_x2[:, :], in_=x2d[:, :]), inc=qS, k=16)
            p.wait("sync", qS)
        phase_barrier()
        ph2.close()
        attn_st.close()

        ph3 = contextlib.ExitStack()
        xn2 = SB(ph3, "xn2", [128, D], BF16)
        xn2F = SB(ph3, "xn2F", [128, 16 * 512], BF16)
        gated = SB(ph3, "gated", [128, 64 * 512], BF16)
        wgT = SB(ph3, "wgT", [128, 16 * 512], BF16)
        wuT = SB(ph3, "wuT", [128, 16 * 512], BF16)
        wdT = [SB(ph3, "wdT0", [128, 512], BF16), SB(ph3, "wdT1", [128, 512], BF16)]
        sg = SB(ph3, "sg", [128, 512], BF16)
        ffo = SB(ph3, "ffo", [128, 4 * D], F32)
        x3t = SB(ph3, "x3t", [128, D], F32)
        sq3 = SB(ph3, "sq3", [128, D], F32)
        yt = SB(ph3, "yt", [128, D], F32)
        ps_big = PS(ph3, "ps3_big", [128, S], F32)
        cell["ps_tr"] = PS(ph3, "ps3_tr", [128, 1024], BF16)
        cell["ps_tr2"] = PS(ph3, "ps3_tr2", [128, 1024], BF16)
        ps_g = PS(ph3, "ps3_g", [128, 512], F32)
        ps_u = PS(ph3, "ps3_u", [128, 512], F32)

        p.wait("sync", qT, stash_done)
        x2_stores = p.n(qS)
        p.wait("scalar", qS, x2_stores)

        gu_done = 0
        for th in range(2):
            for tt in range(4):
                row0 = th * 512 + tt * 128
                p.wait("scalar", tV, ln_state["last_norm"].get("x3", 0))
                p.wait("scalar", qO)
                p.emit("scalar", lambda e, d=x3t[:, :], s=x2d[row0:row0 + 128, :]:
                       e.dma_start(out=d, in_=s), inc=qA, k=16)
                p.wait("vector", qA)
                p.wait("scalar", qA)

                def norm2():
                    p.emit("vector", lambda e: e.tensor_scalar(xn2[:, :], x3t[:, :], stat[:, 8:9],
                                                               stat[:, 20:21], ALU.subtract, ALU.mult), inc=tV)
                    ln_state["last_norm"]["x3"] = p.n(tV)
                ln_stats_norm(x3t[:, :], sq3[:, :], norm2)
                transpose_128(lambda i: xn2[:, i * 128:(i + 1) * 128], 16,
                              lambda i, tt=tt: xn2F[:, i * 512 + tt * 128: i * 512 + tt * 128 + 128], ident)

            for hc in range(16):
                r0 = hc * 8192
                p.wait("sync", tT, gu_done)
                p.emit("sync", lambda e, d=wgT[:, :], s=wgs[r0:r0 + 8192, :]:
                       e.dma_start(out=d, in_=s, transpose=True), inc=qS, k=16)
                p.emit("sync", lambda e, d=wuT[:, :], s=wus[r0:r0 + 8192, :]:
                       e.dma_start(out=d, in_=s, transpose=True), inc=qS, k=16)
                p.wait("tensor", qS)
                if _DEBUG and th == 0 and hc == 0:
                    for cc2 in range(4):
                        p.wait("vector", qS)
                        p.emit("vector", lambda e, cc2=cc2: e.tensor_copy(out=sq3[:, 0:2048],
                                                                          in_=wgT[:, cc2 * 2048:(cc2 + 1) * 2048]), inc=tV)
                        p.wait("sync", tV)
                        p.emit("sync", lambda e, cc2=cc2: e.dma_start(out=dbg_wgT[:, cc2 * 2048:(cc2 + 1) * 2048],
                                                                      in_=sq3[:, 0:2048]), inc=qS, k=16)
                    p.wait("vector", qS)
                    for cc2 in range(4):
                        p.emit("vector", lambda e, cc2=cc2: e.tensor_copy(out=sq3[:, 0:2048],
                                                                          in_=xn2F[:, cc2 * 2048:(cc2 + 1) * 2048]), inc=tV)
                        p.wait("sync", tV)
                        p.emit("sync", lambda e, cc2=cc2: e.dma_start(out=dbg_xnf2[:, cc2 * 2048:(cc2 + 1) * 2048],
                                                                      in_=sq3[:, 0:2048]), inc=qS, k=16)
                        p.wait("vector", qS)
                for hl in range(4):
                    proj_16(lambda dc, hl=hl: wgT[:, dc * 512 + hl * 128: dc * 512 + hl * 128 + 128],
                            lambda dc: xn2F[:, dc * 512:(dc + 1) * 512], ps_g, inc_last=False)
                    proj_16(lambda dc, hl=hl: wuT[:, dc * 512 + hl * 128: dc * 512 + hl * 128 + 128],
                            lambda dc: xn2F[:, dc * 512:(dc + 1) * 512], ps_u)
                    p.wait("scalar", tT)
                    p.wait("scalar", tV)
                    p.emit("scalar", lambda e: e.activation(sg[:, :], ps_g[:, :], AF.Silu), inc=tA)
                    p.wait("vector", tA)
                    hch = hc * 4 + hl
                    p.emit("vector", lambda e, d=gated[:, hch * 512:(hch + 1) * 512]:
                           e.tensor_mul(d, sg[:, :], ps_u[:, :]), inc=tV)
                    p.wait("tensor", tV)
                gu_done = p.n(tT)
                if _DEBUG and th == 0 and hc == 1:
                    for cc2 in range(2):
                        p.emit("vector", lambda e, cc2=cc2: e.tensor_copy(out=sq3[:, 0:2048],
                                                                          in_=gated[:, cc2 * 2048:(cc2 + 1) * 2048]), inc=tV)
                        p.wait("sync", tV)
                        p.emit("sync", lambda e, cc2=cc2: e.dma_start(out=dbg_gated[:, cc2 * 2048:(cc2 + 1) * 2048],
                                                                      in_=sq3[:, 0:2048]), inc=qS, k=16)
                        p.wait("vector", qS)

            dn_n = {0: 0, 1: 0}
            for dh in range(2):
                for dq in range(2):
                    c0 = dh * 1024 + dq * 512
                    for hch in range(64):
                        wpar = hch % 2
                        wdt_ = wdT[wpar]
                        p.wait("sync", tT, dn_n[wpar])
                        p.emit("sync", lambda e, d=wdt_[:, :], s=wdT_d[hch * 128:(hch + 1) * 128, c0:c0 + 512]:
                               e.dma_start(out=d, in_=s), inc=qS, k=16)
                        p.wait("tensor", qS)
                        for tt in range(4):
                            lw = gated[:, hch * 512 + tt * 128: hch * 512 + tt * 128 + 128]
                            p.emit("tensor", lambda e, w=lw: e.ldweights(w))
                            p.emit("tensor", lambda e, o=ps_big[:, tt * 512:(tt + 1) * 512], w=lw,
                                   r2=wdt_[:, :], s=(hch == 0), t=(hch == 63):
                                   e.matmul(o, w, r2, start=s, stop=t), inc=(tT if tt == 3 else None))
                        dn_n[wpar] = p.n(tT)
                    p.wait("vector", tT)
                    for tt in range(4):
                        p.emit("vector", lambda e, s=ps_big[:, tt * 512:(tt + 1) * 512],
                               d=ffo[:, tt * D + c0: tt * D + c0 + 512]:
                               e.tensor_copy(out=d, in_=s), inc=(tV if tt == 3 else None))
                    p.wait("tensor", tV)

            if _DEBUG and th == 0:
                for cc2 in range(4):
                    p.wait("sync", tV)
                    p.emit("sync", lambda e, cc2=cc2: e.dma_start(out=dbg_ffo[:, cc2 * D:(cc2 + 1) * D],
                                                                  in_=ffo[:, cc2 * D:(cc2 + 1) * D]), inc=qS, k=16)
                p.wait("scalar", qS)
                p.wait("vector", qS)
            for tt in range(4):
                row0 = th * 512 + tt * 128
                fv = ffo[:, tt * D:(tt + 1) * D]
                p.wait("scalar", qO)
                p.emit("scalar", lambda e, d=x3t[:, :], s=x2d[row0:row0 + 128, :]:
                       e.dma_start(out=d, in_=s), inc=qA, k=16)

                def fin3(fv=fv, row0=row0):
                    p.emit("vector", lambda e, a=fv: e.tensor_scalar(a, a, stat[:, 8:9], stat[:, 20:21],
                                                                     ALU.subtract, ALU.mult), inc=tV)
                    p.wait("vector", tV)
                    p.wait("vector", qA)
                    p.wait("vector", qO)
                    p.emit("vector", lambda e, a=fv: e.tensor_add(yt[:, :], x3t[:, :], a), inc=tV)
                    p.wait("sync", tV)
                    p.emit("sync", lambda e, d=y[row0:row0 + 128, :], s=yt[:, :]:
                           e.dma_start(out=d, in_=s), inc=qO, k=16)
                ln_stats_norm(fv, sq3[:, :], fin3)

        ph3.close()
        p.wait("sync", qO, p.n(qO))

        p.realize(root.enter_context(nc.Block()))
    return nc


def _shard_inputs(inputs):
    t = _host_tables()
    x = np.ascontiguousarray(np.asarray(inputs["L0_attn_x"], dtype=np.float32))
    per_core = []
    shared = {
        "Wq": np.ascontiguousarray(np.asarray(inputs["L0_attn_Wq"], np.float32)),
        "Wkv": np.ascontiguousarray(np.asarray(inputs["L0_attn_Wkv"], np.float32)),
        "Wo": np.ascontiguousarray(np.asarray(inputs["L0_attn_Wo"], np.float32)),
        "bo": np.ascontiguousarray(np.asarray(inputs["L0_attn_bo"], np.float32).reshape(1, D)),
        "wg": np.ascontiguousarray(np.asarray(inputs["L0_ffn_w_gate"], np.float32)),
        "wu": np.ascontiguousarray(np.asarray(inputs["L0_ffn_w_up"], np.float32)),
        "wd": np.ascontiguousarray(np.asarray(inputs["L0_ffn_w_down"], np.float32)),
        "T_A": t["T_A"],
        "identin": t["ident16"], "ident32in": t["ident32"],
        "onesrow": t["ones_row"], "onescol": t["ones_col16"], "sq128": t["sqrt128"],
    }
    for c in range(8):
        b, hh = c // 2, c % 2
        roll = np.concatenate([np.arange(hh * NTOK, (hh + 1) * NTOK),
                               np.arange((1 - hh) * NTOK, (2 - hh) * NTOK)])
        em = np.ones((128, 8), np.float32)
        kc = np.ones((128, 8), np.float32)
        kc2 = np.ones((128, 1), np.float32)
        if hh == 0:
            em[0:64, 0] = 0.0
            kc[:, :] = 0.0
            kc[0:64, 0] = 1.0
            kc2[:, :] = 1.0
        else:
            kc[:, :] = 1.0
            kc2[:, :] = 0.0
        m = dict(shared)
        m.update({
            "x_own": np.ascontiguousarray(x[b, hh * NTOK:(hh + 1) * NTOK, :]),
            "x_oth": np.ascontiguousarray(x[b, (1 - hh) * NTOK:(2 - hh) * NTOK, :]),
            "T_X": (t["T_A"] if hh == 0 else t["T_B"]),
            "sinq_hi": np.ascontiguousarray(t["sin_hi"][:, hh * NTOK:(hh + 1) * NTOK]),
            "sinq_lo": np.ascontiguousarray(t["sin_lo"][:, hh * NTOK:(hh + 1) * NTOK]),
            "cosq_hi": np.ascontiguousarray(t["cos_hi"][:, hh * NTOK:(hh + 1) * NTOK]),
            "cosq_lo": np.ascontiguousarray(t["cos_lo"][:, hh * NTOK:(hh + 1) * NTOK]),
            "sink_hi": np.ascontiguousarray(t["sin_hi"][:, roll]),
            "sink_lo": np.ascontiguousarray(t["sin_lo"][:, roll]),
            "cosk_hi": np.ascontiguousarray(t["cos_hi"][:, roll]),
            "cosk_lo": np.ascontiguousarray(t["cos_lo"][:, roll]),
            "expmask": em, "keepcol": kc, "keepcol2": kc2,
        })
        per_core.append(m)
    return per_core


def kernel(**inputs):
    per_core = _shard_inputs(inputs)
    nc = build_program()
    res = run_bass_kernel_spmd(nc, per_core, core_ids=list(range(8)))
    out = np.empty((B, S, D), np.float32)
    for c in range(8):
        b, hh = c // 2, c % 2
        out[b, hh * NTOK:(hh + 1) * NTOK, :] = res.results[c]["y"]
    return out


# version 09
# speedup vs baseline: 1.0955x; 1.0955x over previous
import contextlib
import numpy as np
import ml_dtypes
import concourse.bass as bass
import concourse.mybir as mybir
from concourse.bass_utils import run_bass_kernel_spmd

F32 = mybir.dt.float32
BF16 = mybir.dt.bfloat16
AF = mybir.ActivationFunctionType
ALU = mybir.AluOpType
AX = mybir.AxisListType

B, S, D = 4, 2048, 2048
H, KVH, HD, HALF = 16, 8, 128, 64
DH = 8192
NTOK = 1024
NEG = np.float32(-1.0e9)

_cache = {}
_DEBUG = False


def _host_tables():
    if "t" in _cache:
        return _cache["t"]
    base32 = np.float32(2.0) ** (-(np.float32(2.0) ** np.float32(-1.0)))
    slopes32 = (np.float64(base32) ** np.arange(1, 17, dtype=np.float64)).astype(np.float32)
    au = np.abs(np.arange(-2047, 2048))
    E0 = (slopes32[:, None] * (-au[None, :]).astype(np.float32)).astype(np.float32)
    E0[:, au <= 256] = NEG
    Epad = np.concatenate([np.zeros((16, 127), np.float32), E0], axis=1)
    idx = np.clip((np.arange(4095)[None, :] - np.arange(128)[:, None]) + 127, 0, Epad.shape[1] - 1)
    T_A = np.ascontiguousarray(Epad[:, idx])
    idx2 = (np.arange(4095)[None, :] - np.arange(128)[:, None]) - 2048 + 127
    ok = (idx2 >= 127) & (idx2 < Epad.shape[1])
    T_B = np.ascontiguousarray(Epad[:, np.clip(idx2, 0, Epad.shape[1] - 1)] *
                               ok[None, :, :].astype(np.float32))

    freqs = 1.0 / (10000.0 ** (np.arange(HALF, dtype=np.float64) / HALF))
    ph = freqs[:, None] * np.arange(S, dtype=np.float64)[None, :]

    def hilo(v):
        hi = v.astype(np.float32)
        lo = (v - hi.astype(np.float64)).astype(np.float32)
        return np.ascontiguousarray(hi), np.ascontiguousarray(lo)
    sin_hi, sin_lo = hilo(np.sin(ph))
    cos_hi, cos_lo = hilo(np.cos(ph))
    _cache["t"] = dict(
        T_A=T_A, T_B=T_B, sin_hi=sin_hi, sin_lo=sin_lo, cos_hi=cos_hi, cos_lo=cos_lo,
        ident16=np.eye(128, dtype=ml_dtypes.bfloat16),
        ident32=np.eye(128, dtype=np.float32),
        ones_row=np.ones((1, 128), np.float32),
        ones_col16=np.ones((128, 1), ml_dtypes.bfloat16),
        sqrt128=np.sqrt(np.float32(128.0)).astype(np.float32).reshape(1, 1),
    )
    return _cache["t"]


class P:
    ENGS = ("sync", "tensor", "vector", "scalar", "gpsimd")

    def __init__(self, nc, es):
        self.nc = nc
        self.q = {e: [] for e in self.ENGS}
        self.sems = {}
        for name in ("qS", "qA", "qG", "qT", "tT", "tV", "tA", "tG", "qO"):
            h = es.enter_context(nc.semaphore(name))
            self.sems[name] = [h, 0]
        self.issuer = {"qS": "sync", "qA": "scalar", "qG": "gpsimd", "qT": "gpsimd", "qO": "sync"}
        self.boundaries = {k: set() for k in self.issuer}

    def n(self, name):
        return self.sems[name][1]

    def emit(self, eng, fn, inc=None, k=1):
        if inc is None:
            self.q[eng].append(fn)
            return None
        s = self.sems[inc]
        s[1] += k
        self.q[eng].append(lambda e, fn=fn, h=s[0], k=k: fn(e).then_inc(h, k))
        return s[1]

    def wait(self, eng, name, val=None):
        s = self.sems[name]
        v = s[1] if val is None else val
        if v <= 0:
            return
        iss = self.issuer.get(name)
        if iss is not None and eng != iss and v not in self.boundaries[name]:
            self.boundaries[name].add(v)
            self.q[iss].append(lambda e, h=s[0], v=v: e.wait_ge(h, v))
        self.q[eng].append(lambda e, h=s[0], v=v: e.wait_ge(h, v))

    def realize(self, block):
        for eng in self.ENGS:
            ops = self.q[eng]

            def runner(engobj, ops=ops):
                for f in ops:
                    f(engobj)
            getattr(block, eng)(runner)


def build_program():
    nc = bass.Bass("TRN2", target_bir_lowering=False, debug=False)
    dt = nc.dram_tensor

    x_own = dt("x_own", [NTOK, D], F32, kind="ExternalInput").ap()
    x_oth = dt("x_oth", [NTOK, D], F32, kind="ExternalInput").ap()
    Wq = dt("Wq", [D, D], F32, kind="ExternalInput").ap()
    Wkv = dt("Wkv", [D, D], F32, kind="ExternalInput").ap()
    Wo = dt("Wo", [D, D], F32, kind="ExternalInput").ap()
    bo = dt("bo", [1, D], F32, kind="ExternalInput").ap()
    wg = dt("wg", [DH, D], F32, kind="ExternalInput").ap()
    wu = dt("wu", [DH, D], F32, kind="ExternalInput").ap()
    wd = dt("wd", [D, DH], F32, kind="ExternalInput").ap()
    T_A = dt("T_A", [16, 128, 4095], F32, kind="ExternalInput").ap()
    T_X = dt("T_X", [16, 128, 4095], F32, kind="ExternalInput").ap()
    tabs_in = {}
    for nm in ("sinq_hi", "sinq_lo", "cosq_hi", "cosq_lo"):
        tabs_in[nm] = dt(nm, [HALF, NTOK], F32, kind="ExternalInput").ap()
    for nm in ("sink_hi", "sink_lo", "cosk_hi", "cosk_lo"):
        tabs_in[nm] = dt(nm, [HALF, S], F32, kind="ExternalInput").ap()
    identin = dt("identin", [128, 128], BF16, kind="ExternalInput").ap()
    ident32in = dt("ident32in", [128, 128], F32, kind="ExternalInput").ap()
    onesrow = dt("onesrow", [1, 128], F32, kind="ExternalInput").ap()
    onescol = dt("onescol", [128, 1], BF16, kind="ExternalInput").ap()
    sq128 = dt("sq128", [1, 1], F32, kind="ExternalInput").ap()
    expmask = dt("expmask", [128, 8], F32, kind="ExternalInput").ap()
    keepcol = dt("keepcol", [128, 8], F32, kind="ExternalInput").ap()
    keepcol2 = dt("keepcol2", [128, 1], F32, kind="ExternalInput").ap()
    y = dt("y", [NTOK, D], F32, kind="ExternalOutput").ap()
    wgs = dt("wgs", [16 * 16 * 512, 128], BF16, kind="Internal").ap()
    wus = dt("wus", [16 * 16 * 512, 128], BF16, kind="Internal").ap()
    wds = dt("wds", [D, DH], BF16, kind="Internal").ap()
    wdT_d = dt("wdT_d", [DH, D], BF16, kind="Internal").ap()
    x2d = dt("x2d", [NTOK, D], F32, kind="Internal").ap()
    if _DEBUG:
        dbg_x2 = dt("dbg_x2", [NTOK, D], F32, kind="ExternalOutput").ap()
        dbg_xnf = dt("dbg_xnf", [128, 16 * 512], F32, kind="ExternalOutput").ap()
        dbg_xt = dt("dbg_xt", [128, D], F32, kind="ExternalOutput").ap()
        dbg_stat = dt("dbg_stat", [128, 24], F32, kind="ExternalOutput").ap()
        dbg_st4 = dt("dbg_st4", [128, 96], F32, kind="ExternalOutput").ap()
        dbg_xn = dt("dbg_xn", [128, D], F32, kind="ExternalOutput").ap()
        dbg_kps = dt("dbg_kps", [128, 512], F32, kind="ExternalOutput").ap()
        dbg_tab = dt("dbg_tab", [64, 4 * S], F32, kind="ExternalOutput").ap()
        dbg_wgT = dt("dbg_wgT", [128, 16 * 512], F32, kind="ExternalOutput").ap()
        dbg_xnf2 = dt("dbg_xnf2", [128, 16 * 512], F32, kind="ExternalOutput").ap()
        dbg_gated = dt("dbg_gated", [128, 8 * 512], F32, kind="ExternalOutput").ap()
        dbg_ffo = dt("dbg_ffo", [128, 4 * D], F32, kind="ExternalOutput").ap()
        dbg_k = dt("dbg_k", [128, KVH * S], F32, kind="ExternalOutput").ap()
        dbg_q = dt("dbg_q", [128, H * NTOK], F32, kind="ExternalOutput").ap()
        dbg_v = dt("dbg_v", [128, 16 * 1024], F32, kind="ExternalOutput").ap()
        dbg_af = dt("dbg_af", [128, H * 512], F32, kind="ExternalOutput").ap()

    root = contextlib.ExitStack()
    with root:
        def SB(st, name, shape, dty):
            return st.enter_context(nc.sbuf_tensor(name, shape, dty))

        def PS(st, name, shape, dty=F32):
            return st.enter_context(nc.psum_tensor(name, shape, dty))

        p = P(nc, root)
        qS, qA, qG, tT, tV, tA, tG, qO = "qS", "qA", "qG", "tT", "tV", "tA", "tG", "qO"

        ident = SB(root, "ident", [128, 128], BF16)
        ident32 = SB(root, "ident32", [128, 128], F32)
        ones_r = SB(root, "ones_r", [1, 128], F32)
        ones_c = SB(root, "ones_c", [128, 1], BF16)
        rmask = SB(root, "rmask", [128, 8], F32)
        kcol = SB(root, "kcol", [128, 8], F32)
        kcol2 = SB(root, "kcol2", [128, 1], F32)
        eps_t = SB(root, "eps_t", [128, 1], F32)
        sc_rec = SB(root, "sc_rec", [1, 1], F32)
        stat = SB(root, "stat", [128, 24], F32)
        attn_st = contextlib.ExitStack()
        K_sb = SB(attn_st, "K_sb", [128, KVH * S], BF16)
        V_sb = SB(attn_st, "V_sb", [128, 16 * 1024], BF16)
        q_sb = SB(attn_st, "q_sb", [128, H * NTOK], BF16)
        cosC = SB(attn_st, "cosC", [HALF, NTOK], BF16)
        sinC = SB(attn_st, "sinC", [HALF, NTOK], BF16)

        cell = {}

        def phase_barrier():
            for eng in ("sync", "gpsimd", "scalar", "vector", "tensor"):
                for sem in (tT, tV, tA, tG):
                    p.wait(eng, sem)

        def load_wcols(w_ap, c0, ncols):
            wt = cell["wt"]
            par = wt["par"]
            wt["par"] ^= 1
            tile = wt["tiles"][par]
            p.wait("gpsimd", tT, wt["stamp"][par])
            src = w_ap[:, c0:c0 + ncols].rearrange("(dc q) c -> q dc c", q=128)
            dstv = tile[:, 0:16 * ncols].rearrange("q (dc c) -> q dc c", c=ncols)
            p.emit("gpsimd", lambda e, a=dstv, b=src: e.dma_start(out=a, in_=b), inc=qG, k=16)
            wt["ready"][par] = p.n(qG)
            wt["cur"] = par
            return tile

        def wt_ready():
            wt = cell["wt"]
            p.wait("tensor", qG, wt["ready"][wt["cur"]])

        def wt_used():
            wt = cell["wt"]
            wt["stamp"][wt["cur"]] = p.n(tT)

        def proj_16(lhs_fn, rhs_fn, pst, inc_last=True):
            if not isinstance(pst, bass.AP):
                pst = pst[:, :]
            for dc in range(16):
                w = lhs_fn(dc)
                p.emit("tensor", lambda e, w=w: e.ldweights(w))
                p.emit("tensor", lambda e, o=pst, w=w, r=rhs_fn(dc), s=(dc == 0), t=(dc == 15):
                       e.matmul(o, w, r, start=s, stop=t),
                       inc=(tT if (dc == 15 and inc_last) else None))

        def transpose_128(src_fn, n_t, dst_fn, idmat):
            if not isinstance(idmat, bass.AP):
                idmat = idmat[:, :]
            p.wait("tensor", tV)
            ptr, ptr2 = cell["ps_tr"], cell["ps_tr2"]
            done = 0
            while done < n_t:
                grp = min(8, n_t - done)
                pst = ptr if (done // 8) % 2 == 0 else ptr2
                for i in range(grp):
                    p.emit("tensor", lambda e, s=src_fn(done + i), d=pst[:, i * 128:(i + 1) * 128]:
                           e.transpose(d, s, idmat), inc=(tT if i == grp - 1 else None))
                p.wait("vector", tT)
                for i in range(grp):
                    p.emit("vector", lambda e, s=pst[:, i * 128:(i + 1) * 128], d=dst_fn(done + i):
                           e.tensor_copy(out=d, in_=s), inc=(tV if i == grp - 1 else None))
                p.wait("tensor", tV)
                done += grp

        ln_state = {"last_sq_red": 0, "last_norm": {}}

        def ln_stats_norm(src_tile, sqbuf, out_fn):
            p.wait("vector", tV)
            p.emit("vector", lambda e, a=src_tile: e.reduce_sum(stat[:, 0:1], a, axis=AX.X), inc=tV)
            p.wait("scalar", tV)
            p.wait("scalar", qO)
            p.emit("scalar", lambda e, a=src_tile, q=sqbuf: e.activation(q, a, AF.Square), inc=tA)
            p.wait("vector", tA)
            p.emit("vector", lambda e, q=sqbuf: e.reduce_sum(stat[:, 16:17], q, axis=AX.X), inc=tV)
            ln_state["last_sq_red"] = p.n(tV)
            p.wait("vector", tV)
            p.emit("vector", lambda e: e.tensor_scalar(stat[:, 8:9], stat[:, 0:1], 1.0 / D, None, ALU.mult), inc=tV)
            p.wait("vector", tV)
            p.emit("vector", lambda e: e.tensor_mul(stat[:, 12:13], stat[:, 8:9], stat[:, 8:9]), inc=tV)
            p.wait("vector", tV)
            p.emit("vector", lambda e: e.scalar_tensor_tensor(stat[:, 16:17], stat[:, 16:17], 1.0 / D,
                                                              stat[:, 12:13], ALU.mult, ALU.subtract), inc=tV)
            p.wait("scalar", tV)
            p.emit("scalar", lambda e: e.activation(stat[:, 20:21], stat[:, 16:17], AF.Ln, bias=eps_t[:]), inc=tA)
            p.wait("scalar", tA)
            p.emit("scalar", lambda e: e.activation(stat[:, 20:21], stat[:, 20:21], AF.Exp, scale=-0.5), inc=tA)
            p.wait("vector", tA)
            p.wait("vector", tV)
            out_fn()

        ph1 = contextlib.ExitStack()
        cosk = SB(ph1, "cosk", [HALF, S], BF16)
        sink = SB(ph1, "sink", [HALF, S], BF16)
        cosq = SB(ph1, "cosq", [HALF, NTOK], BF16)
        sinq = SB(ph1, "sinq", [HALF, NTOK], BF16)
        c_row = SB(ph1, "c_row", [1, NTOK], F32)
        x_t = [SB(ph1, "x_t0", [128, D], F32), SB(ph1, "x_t1", [128, D], F32)]
        xn1 = SB(ph1, "xn1", [128, D], BF16)
        xn1F = SB(ph1, "xn1F", [128, 16 * 512], BF16)
        cell["wt"] = {"tiles": [SB(ph1, "wtile1a", [128, 16 * 512], BF16),
                                SB(ph1, "wtile1b", [128, 16 * 512], BF16)],
                      "par": 0, "stamp": [0, 0], "ready": [0, 0], "cur": 0}
        sq_s = SB(ph1, "sq_s", [128, D], F32)
        rt = [SB(ph1, f"rt{i}", [HALF, 512], F32) for i in range(4)]
        csb = SB(ph1, "csb", [HALF, 512], F32)
        c_col = SB(ph1, "c_col", [128, 4], F32)
        hi_sb = x_t[0][0:HALF, 0:S]
        lo_sb = x_t[1][0:HALF, 0:S]
        ps_main = PS(ph1, "ps1_main", [128, S], F32)
        cell["ps_tr"] = PS(ph1, "ps1_tr", [128, 1024], BF16)
        cell["ps_tr2"] = PS(ph1, "ps1_tr2", [128, 1024], BF16)
        ps_aux = PS(ph1, "ps1_aux", [128, 512], F32)
        ps_p = [ps_main[:, 0:512], ps_main[:, 512:1024]]

        wgv = wg.rearrange("(hc hl) (dc q) -> hc dc hl q", hl=512, q=128)
        wuv = wu.rearrange("(hc hl) (dc q) -> hc dc hl q", hl=512, q=128)
        wgsv = wgs.rearrange("(hc dc hl) q -> hc dc hl q", dc=16, hl=512)
        wusv = wus.rearrange("(hc dc hl) q -> hc dc hl q", dc=16, hl=512)
        qT = "qT"
        for hc in range(16):
            p.emit("gpsimd", lambda e, a=wgsv[hc], b=wgv[hc]:
                   e.dma_start(out=a, in_=b), inc=qT, k=16)
            p.emit("gpsimd", lambda e, a=wusv[hc], b=wuv[hc]:
                   e.dma_start(out=a, in_=b), inc=qT, k=16)
        for dc in range(8):
            r0 = dc * 256
            p.emit("gpsimd", lambda e, a=wds[r0:r0 + 256, :], b=wd[r0:r0 + 256, :]:
                   e.dma_start(out=a, in_=b), inc=qT, k=16)
        stash_done = p.n(qT)

        for dst, hin, lin, n in ((cosk, "cosk_hi", "cosk_lo", S), (sink, "sink_hi", "sink_lo", S),
                                 (cosq, "cosq_hi", "cosq_lo", NTOK), (sinq, "sinq_hi", "sinq_lo", NTOK)):
            p.emit("sync", lambda e, a=hi_sb[:, 0:n], b=tabs_in[hin]: e.dma_start(out=a, in_=b), inc=qS, k=16)
            p.emit("sync", lambda e, a=lo_sb[:, 0:n], b=tabs_in[lin]: e.dma_start(out=a, in_=b), inc=qS, k=16)
            p.wait("vector", qS)
            p.emit("vector", lambda e, d=dst[:, 0:n], a=hi_sb[:, 0:n], b=lo_sb[:, 0:n]:
                   e.tensor_add(d, a, b), inc=tV)
            p.wait("sync", tV)
        for dsb, din in ((ident, identin), (ident32, ident32in), (ones_r, onesrow),
                         (ones_c, onescol), (sc_rec, sq128), (rmask, expmask),
                         (kcol, keepcol), (kcol2, keepcol2)):
            p.emit("sync", lambda e, a=dsb[:, :], b=din: e.dma_start(out=a, in_=b), inc=qS, k=16)
        p.wait("vector", qS)
        p.emit("vector", lambda e: e.memset(eps_t[:], 1e-5), inc=tV)
        p.emit("vector", lambda e: e.reciprocal(sc_rec[:], sc_rec[:]), inc=tV)
        p.wait("tensor", qS)
        p.wait("scalar", qS)
        p.wait("gpsimd", qS)
        p.wait("sync", tV)
        wdstg = [SB(ph1, "wdstg0", [128, 1024], BF16), SB(ph1, "wdstg1", [128, 1024], BF16)]
        wd_tr_n = {0: 0, 1: 0}
        p.wait("sync", qT, stash_done)
        for hch in range(64):
            for dhf in range(2):
                par = (hch * 2 + dhf) % 2
                p.wait("sync", qS, wd_tr_n[par])
                p.emit("sync", lambda e, d=wdstg[par][:, :],
                       s=wds[dhf * 1024:(dhf + 1) * 1024, hch * 128:(hch + 1) * 128]:
                       e.dma_start(out=d, in_=s, transpose=True), inc=qS, k=16)
                p.wait("sync", qS)
                p.emit("sync", lambda e, d=wdT_d[hch * 128:(hch + 1) * 128, dhf * 1024:(dhf + 1) * 1024],
                       s=wdstg[par][:, :]: e.dma_start(out=d, in_=s), inc=qS, k=16)
                wd_tr_n[par] = p.n(qS)
        wd_tr_done = p.n(qS)

        for sp in range(4):
            src = x_own if sp < 2 else x_oth
            base = (sp % 2) * 512
            for tt in range(4):
                par = tt % 2
                xt_ = x_t[par]
                p.wait("scalar", tV, ln_state["last_norm"].get(par, 0))
                p.emit("scalar", lambda e, a=xt_[:, :], b=src[base + tt * 128: base + (tt + 1) * 128, :]:
                       e.dma_start(out=a, in_=b), inc=qA, k=16)
                p.wait("vector", qA)
                p.wait("scalar", qA)

                def norm_out(xt_=xt_, sp=sp, tt=tt):
                    p.emit("vector", lambda e, a=xt_[:, :]:
                           e.tensor_scalar(xn1[:, :], a, stat[:, 8:9], stat[:, 20:21],
                                           ALU.subtract, ALU.mult), inc=tV)
                    ln_state["last_norm"][tt % 2] = p.n(tV)
                    if sp < 2:
                        p.wait("vector", tV)
                        p.emit("vector", lambda e, cc=c_col[:, tt:tt + 1]:
                               e.reduce_sum(cc, xn1[:, :], axis=AX.X, apply_absolute_value=True), inc=tV)
                ln_stats_norm(xt_[:, :], sq_s[:, :], norm_out)
                if _DEBUG and sp == 0:
                    p.emit("vector", lambda e, d_=dbg_st4, tt=tt:
                           e.tensor_copy(out=sq_s[:, 1024 + tt * 24:1024 + (tt + 1) * 24], in_=stat[:, :]), inc=tV)
                    if tt == 0:
                        p.emit("vector", lambda e: e.tensor_copy(out=sq_s[:, 256:256 + D - 256],
                                                                 in_=xn1[:, 0:D - 256]), inc=tV)
                        p.wait("sync", tV)
                        p.emit("sync", lambda e: e.dma_start(out=dbg_xn[:, 0:D - 256],
                                                             in_=sq_s[:, 256:256 + D - 256]), inc=qS, k=16)
                        p.wait("vector", qS)
                if _DEBUG and sp == 0 and tt == 3:
                    p.wait("sync", tV)
                    p.emit("sync", lambda e: e.dma_start(out=dbg_st4[:, :], in_=sq_s[:, 1024:1024 + 96]), inc=qS, k=16)
                    p.wait("vector", qS)
                if _DEBUG and sp == 0 and tt == 0:
                    p.wait("sync", tV)
                    p.emit("sync", lambda e, d=dbg_xt[:, :], s=xt_[:, :]: e.dma_start(out=d, in_=s), inc=qS, k=16)
                    p.emit("sync", lambda e: e.dma_start(out=dbg_stat[:, :], in_=stat[:, :]), inc=qS, k=16)
                    p.wait("vector", qS)
                transpose_128(lambda i: xn1[:, i * 128:(i + 1) * 128], 16,
                              lambda i, tt=tt: xn1F[:, i * 512 + tt * 128: i * 512 + tt * 128 + 128], ident)
            xn1F_ready = p.n(tV)
            if _DEBUG and sp == 0:
                for cc in range(4):
                    p.emit("vector", lambda e, d=sq_s[:, :], s=xn1F[:, cc * 2048:(cc + 1) * 2048]:
                           e.tensor_copy(out=d, in_=s), inc=tV)
                    p.wait("sync", tV)
                    p.emit("sync", lambda e, d=dbg_xnf[:, cc * 2048:(cc + 1) * 2048], s=sq_s[:, :]:
                           e.dma_start(out=d, in_=s), inc=qS, k=16)
                    p.wait("vector", qS)

            if sp < 2:
                p.wait("scalar", tV, xn1F_ready)
                p.emit("scalar", lambda e: e.activation(c_col[:, 0:4], c_col[:, 0:4], AF.Sigmoid), inc=tA)
                p.wait("tensor", tA)
                for tt in range(4):
                    p.emit("tensor", lambda e, s=c_col[:, tt:tt + 1]:
                           e.transpose(ps_aux[0:1, 0:128], s, ident32[:, :]), inc=tT)
                    p.wait("vector", tT)
                    p.emit("vector", lambda e, d=c_row[0:1, sp * 512 + tt * 128: sp * 512 + (tt + 1) * 128]:
                           e.tensor_copy(out=d, in_=ps_aux[0:1, 0:128]), inc=tV)
                    p.wait("tensor", tV)
                p.wait("vector", tV)
                p.emit("vector", lambda e, a=c_row[0:1, sp * 512:(sp + 1) * 512]:
                       e.tensor_scalar(a, a, sc_rec[0:1, 0:1], None, ALU.mult), inc=tV)
                p.wait("tensor", tV)
                p.emit("tensor", lambda e, r=c_row[0:1, sp * 512:(sp + 1) * 512]:
                       e.matmul(ps_aux[0:64, :], ones_r[0:1, 0:64], r, start=True, stop=True), inc=tT)
                p.wait("vector", tT)
                p.emit("vector", lambda e: e.tensor_copy(out=csb[:, :], in_=ps_aux[0:64, :]), inc=tV)
                p.wait("tensor", tV)
                p.wait("vector", tV)
                p.emit("vector", lambda e, d=cosC[:, sp * 512:(sp + 1) * 512],
                       t=cosq[:, sp * 512:(sp + 1) * 512]: e.tensor_mul(d, t, csb[:, :]), inc=tV)
                p.emit("vector", lambda e, d=sinC[:, sp * 512:(sp + 1) * 512],
                       t=sinq[:, sp * 512:(sp + 1) * 512]: e.tensor_mul(d, t, csb[:, :]), inc=tV)

            def rope_evac(pst, tabc, tabs, dst_lo, dst_hi):
                p.wait("vector", tV)
                p.emit("vector", lambda e, a=pst[0:64, :], c=tabc: e.tensor_mul(rt[0][:, :], a, c), inc=tV)
                p.emit("vector", lambda e, a=pst[64:128, :], s2=tabs: e.tensor_mul(rt[1][:, :], a, s2), inc=tV)
                p.emit("vector", lambda e, a=pst[0:64, :], s2=tabs: e.tensor_mul(rt[2][:, :], a, s2), inc=tV)
                p.emit("vector", lambda e, a=pst[64:128, :], c=tabc: e.tensor_mul(rt[3][:, :], a, c), inc=tV)
                p.wait("vector", tV)
                p.emit("vector", lambda e, d=dst_lo: e.tensor_sub(d, rt[0][:, :], rt[1][:, :]), inc=tV)
                p.emit("vector", lambda e, d=dst_hi: e.tensor_add(d, rt[2][:, :], rt[3][:, :]), inc=tV)

            for ocq in range(4):
                wtl = load_wcols(Wkv, ocq * 256, 256)
                wt_ready()
                p.wait("tensor", tV, xn1F_ready)
                for oc2 in range(2):
                    oc = ocq * 2 + oc2
                    proj_16(lambda dc, oc2=oc2, wt=wtl: wt[:, dc * 256 + oc2 * 128: dc * 256 + oc2 * 128 + 128],
                            lambda dc: xn1F[:, dc * 512:(dc + 1) * 512], ps_p[oc2])
                    p.wait("vector", tT)
                    if _DEBUG and sp == 0 and ocq == 0 and oc2 == 0:
                        p.emit("vector", lambda e: e.tensor_copy(out=sq_s[:, 0:512], in_=ps_p[0]), inc=tV)
                        p.wait("sync", tV)
                        p.emit("sync", lambda e: e.dma_start(out=dbg_kps[:, :], in_=sq_s[:, 0:512]), inc=qS, k=16)
                        p.wait("vector", qS)
                    rope_evac(ps_p[oc2], cosk[:, sp * 512:(sp + 1) * 512], sink[:, sp * 512:(sp + 1) * 512],
                              K_sb[0:64, oc * S + sp * 512: oc * S + (sp + 1) * 512],
                              K_sb[64:128, oc * S + sp * 512: oc * S + (sp + 1) * 512])
                wt_used()
                p.wait("tensor", tV)

            for vq in range(2):
                wtl = load_wcols(Wkv, 1024 + vq * 512, 512)
                wt_ready()
                for tt in range(4):
                    if tt >= 2:
                        p.wait("tensor", tV)
                    proj_16(lambda dc, tt=tt: xn1F[:, dc * 512 + tt * 128: dc * 512 + tt * 128 + 128],
                            lambda dc, wt=wtl: wt[:, dc * 512:(dc + 1) * 512], ps_p[tt % 2])
                    p.wait("vector", tT)
                    kt = sp * 4 + tt
                    p.emit("vector", lambda e, s=ps_p[tt % 2],
                           d=V_sb[:, kt * 1024 + vq * 512: kt * 1024 + (vq + 1) * 512]:
                           e.tensor_copy(out=d, in_=s), inc=tV)
                wt_used()
                p.wait("tensor", tV)

            if sp < 2:
                for ocq in range(4):
                    wtl = load_wcols(Wq, ocq * 512, 512)
                    wt_ready()
                    for oc2 in range(4):
                        oc = ocq * 4 + oc2
                        if oc2 >= 2:
                            p.wait("tensor", tV)
                        proj_16(lambda dc, oc2=oc2, wt=wtl:
                                wt[:, dc * 512 + oc2 * 128: dc * 512 + oc2 * 128 + 128],
                                lambda dc: xn1F[:, dc * 512:(dc + 1) * 512], ps_p[oc2 % 2])
                        p.wait("vector", tT)
                        rope_evac(ps_p[oc2 % 2], cosC[:, sp * 512:(sp + 1) * 512],
                                  sinC[:, sp * 512:(sp + 1) * 512],
                                  q_sb[0:64, oc * NTOK + sp * 512: oc * NTOK + (sp + 1) * 512],
                                  q_sb[64:128, oc * NTOK + sp * 512: oc * NTOK + (sp + 1) * 512])
                    wt_used()
                    p.wait("tensor", tV)

        if _DEBUG:
            for ii, tb in enumerate((cosk, sink, cosq, sinq)):
                n2 = S if ii < 2 else NTOK
                p.emit("vector", lambda e, tb=tb, n2=n2: e.tensor_copy(out=hi_sb[:, 0:n2], in_=tb[:, 0:n2]), inc=tV)
                p.wait("sync", tV)
                p.emit("sync", lambda e, ii=ii, n2=n2: e.dma_start(out=dbg_tab[:, ii * S: ii * S + n2],
                                                                   in_=hi_sb[:, 0:n2]), inc=qS, k=16)
                p.wait("vector", qS)
        phase_barrier()
        ph1.close()

        ph2 = contextlib.ExitStack()
        s_sb = [SB(ph2, "s_sb0", [128, S], F32), SB(ph2, "s_sb1", [128, S], F32)]
        p_sb = [SB(ph2, "p_sb0", [128, S], BF16), SB(ph2, "p_sb1", [128, S], BF16)]
        a_single = SB(ph2, "a_t0", [128, S], F32)
        a_t = [a_single, a_single]
        pT_sb = SB(ph2, "pT_sb", [128, 16 * 512], BF16)
        attnF = SB(ph2, "attnF", [128, H * 512], BF16)
        rinvb = SB(ph2, "rinvb", [128, 512], F32)
        rsum = rinvb[0:1, :]
        nmax = SB(ph2, "nmax", [128, 8], F32)
        ebias = SB(ph2, "ebias", [128, 8], F32)
        cell["wt"] = {"tiles": [SB(ph2, "wtile2a", [128, 16 * 128], BF16),
                                SB(ph2, "wtile2b", [128, 16 * 128], BF16)],
                      "par": 0, "stamp": [0, 0], "ready": [0, 0], "cur": 0}
        aout = SB(ph2, "aout", [128, 2 * D], F32)
        x2t = SB(ph2, "x2t", [128, D], F32)
        bo_b = SB(ph2, "bo_b", [128, D], F32)
        sq2 = s_sb[0][:, :]
        ps_main = PS(ph2, "ps2_main", [128, S], F32)
        cell["ps_tr"] = PS(ph2, "ps2_tr", [128, 1024], BF16)
        cell["ps_tr2"] = PS(ph2, "ps2_tr2", [128, 1024], BF16)
        ps_pv = PS(ph2, "ps2_pv", [128, 512], F32)
        ps_aux = PS(ph2, "ps2_aux", [128, 512], F32)
        ps_p = [ps_main[:, 0:512], ps_main[:, 512:1024]]

        bo_sb = x2t[0:1, :]
        p.emit("sync", lambda e: e.dma_start(out=bo_sb, in_=bo), inc=qS, k=16)
        p.wait("tensor", qS)
        for oc in range(4):
            p.emit("tensor", lambda e, r=bo_sb[0:1, oc * 512:(oc + 1) * 512]:
                   e.matmul(ps_aux[:, :], ones_r[0:1, :], r, start=True, stop=True), inc=tT)
            p.wait("vector", tT)
            p.emit("vector", lambda e, d=bo_b[:, oc * 512:(oc + 1) * 512]:
                   e.tensor_copy(out=d, in_=ps_aux[:, :]), inc=tV)
            p.wait("tensor", tV)

        add_n = {0: 0, 1: 0}
        exp_n = {0: 0, 1: 0}
        trans_n = {0: 0, 1: 0}
        for qh in range(2):
            for h in range(H):
                jkv = h // 2
                for qt in range(4):
                    gqt = qh * 4 + qt
                    off = 2047 - gqt * 128
                    r = (qh * H + h) * 4 + qt
                    par = r % 2
                    at_, ss_, pp_ = a_t[par], s_sb[par], p_sb[par]
                    p.wait("sync", tV, max(add_n[0], add_n[1]))
                    p.emit("sync", lambda e, d=at_[:, 0:1024], s=T_A[h, :, off:off + 1024]:
                           e.dma_start(out=d, in_=s), inc=qS, k=16)
                    p.emit("sync", lambda e, d=at_[:, 1024:2048], s=T_X[h, :, off + 1024:off + 2048]:
                           e.dma_start(out=d, in_=s), inc=qS, k=16)
                    qtile = q_sb[:, h * NTOK + gqt * 128: h * NTOK + gqt * 128 + 128]
                    p.emit("tensor", lambda e, w=qtile: e.ldweights(w))
                    for kc in range(4):
                        p.emit("tensor", lambda e, o=ps_main[:, kc * 512:(kc + 1) * 512], w=qtile,
                               r2=K_sb[:, jkv * S + kc * 512: jkv * S + (kc + 1) * 512]:
                               e.matmul(o, w, r2, start=True, stop=True), inc=(tT if kc == 3 else None))
                    p.wait("vector", tT)
                    p.wait("vector", qS)
                    p.wait("vector", tA, exp_n[par])
                    p.emit("vector", lambda e, d=ss_[:, :], a=at_[:, :]:
                           e.tensor_add(d, ps_main[:, :], a), inc=tV)
                    add_n[par] = p.n(tV)
                    p.wait("vector", tV)
                    p.emit("vector", lambda e, s=ss_[:, :], d=nmax[:, par * 4:par * 4 + 1]:
                           e.reduce_max(d, s, axis=AX.X, negate=True), inc=tV)
                    p.wait("vector", tV)
                    p.emit("vector", lambda e, rm=rmask[:, gqt:gqt + 1], d=ebias[:, par * 4:par * 4 + 1],
                           s=nmax[:, par * 4:par * 4 + 1]: e.tensor_mul(d, s, rm), inc=tV)
                    p.wait("scalar", tV)
                    p.wait("scalar", tT, trans_n[par])
                    p.emit("scalar", lambda e, d=pp_[:, :], s=ss_[:, :], sc=rmask[:, gqt:gqt + 1],
                           b=ebias[:, par * 4:par * 4 + 1]:
                           e.activation(d, s, AF.Exp, bias=b, scale=sc), inc=tA)
                    exp_n[par] = p.n(tA)
                    p.wait("gpsimd", tA)
                    p.emit("gpsimd", lambda e, d=pp_, kk=kcol[:, gqt:gqt + 1]:
                           e.tensor_scalar(d[:, 0:64], d[:, 0:64], kk, None, ALU.mult), inc=tG)
                    p.emit("gpsimd", lambda e, d=pp_: e.tensor_scalar(d[:, 1024:1088], d[:, 1024:1088],
                                                                     kcol2[:, 0:1], None, ALU.mult), inc=tG)
                    p.wait("tensor", tG)
                    p.wait("tensor", tV, add_n[par])
                    for halfg in range(2):
                        pst = cell["ps_tr"] if halfg == 0 else cell["ps_tr2"]
                        for i in range(8):
                            kt = halfg * 8 + i
                            p.emit("tensor", lambda e, s=pp_[:, kt * 128:(kt + 1) * 128],
                                   d=pst[:, i * 128:(i + 1) * 128]:
                                   e.transpose(d, s, ident[:, :]), inc=(tT if i == 7 else None))
                        p.wait("vector", tT)
                        for i in range(8):
                            kt = halfg * 8 + i
                            p.emit("vector", lambda e, s=pst[:, i * 128:(i + 1) * 128],
                                   d=pT_sb[:, kt * 512 + qt * 128: kt * 512 + qt * 128 + 128]:
                                   e.tensor_copy(out=d, in_=s), inc=(tV if i == 7 else None))
                        p.wait("tensor", tV)
                    trans_n[par] = p.n(tT)
                p.emit("tensor", lambda e: e.ldweights(ones_c[:, :]))
                for kt in range(16):
                    p.emit("tensor", lambda e, r2=pT_sb[:, kt * 512:(kt + 1) * 512], s=(kt == 0), t=(kt == 15):
                           e.matmul(ps_aux[0:1, :], ones_c[:, :], r2, start=s, stop=t),
                           inc=(tT if kt == 15 else None))
                p.wait("vector", tT)
                p.emit("vector", lambda e: e.tensor_copy(out=rsum[:, :], in_=ps_aux[0:1, :]), inc=tV)
                p.wait("vector", tV)
                p.emit("vector", lambda e: e.reciprocal(rsum[:, :], rsum[:, :]), inc=tV)
                p.wait("tensor", tV)
                p.emit("tensor", lambda e: e.matmul(ps_aux[:, :], ones_r[0:1, :], rsum[0:1, :],
                                                    start=True, stop=True), inc=tT)
                p.wait("vector", tT)
                p.emit("vector", lambda e: e.tensor_copy(out=rinvb[:, :], in_=ps_aux[:, :]), inc=tV)
                p.wait("tensor", tV)
                p.wait("vector", tV)
                for kt in range(16):
                    vt = V_sb[:, kt * 1024 + jkv * 128: kt * 1024 + jkv * 128 + 128]
                    p.emit("tensor", lambda e, w=vt: e.ldweights(w))
                    p.emit("tensor", lambda e, w=vt, r2=pT_sb[:, kt * 512:(kt + 1) * 512],
                           s=(kt == 0), t=(kt == 15):
                           e.matmul(ps_pv[:, :], w, r2, start=s, stop=t), inc=(tT if kt == 15 else None))
                p.wait("vector", tT)
                p.emit("vector", lambda e, d=attnF[:, h * 512:(h + 1) * 512]:
                       e.tensor_mul(d, ps_pv[:, :], rinvb[:, :]), inc=tV)
                p.wait("tensor", tV)

            for pair in range(2):
                for oc in range(16):
                    wtl = load_wcols(Wo, oc * 128, 128)
                    wt_ready()
                    for t2 in range(2):
                        tt = pair * 2 + t2
                        proj_16(lambda dc, tt=tt: attnF[:, dc * 512 + tt * 128: dc * 512 + tt * 128 + 128],
                                lambda dc, wt=wtl: wt[:, dc * 128:(dc + 1) * 128], ps_p[t2][:, 0:128])
                        p.wait("vector", tT)
                        p.emit("vector", lambda e, s=ps_p[t2][:, 0:128],
                               d=aout[:, t2 * D + oc * 128: t2 * D + (oc + 1) * 128],
                               b=bo_b[:, oc * 128:(oc + 1) * 128]: e.tensor_add(d, s, b), inc=tV)
                        p.wait("tensor", tV)
                    wt_used()
                for t2 in range(2):
                    tt = pair * 2 + t2
                    row0 = qh * 512 + tt * 128
                    av = aout[:, t2 * D:(t2 + 1) * D]
                    p.wait("scalar", qS)
                    p.emit("scalar", lambda e, d=x2t[:, :], s=x_own[row0:row0 + 128, :]:
                           e.dma_start(out=d, in_=s), inc=qA, k=16)

                    def fin2(av=av, row0=row0):
                        p.emit("vector", lambda e, a=av: e.tensor_scalar(a, a, stat[:, 8:9], stat[:, 20:21],
                                                                         ALU.subtract, ALU.mult), inc=tV)
                        p.wait("vector", tV)
                        p.wait("vector", qA)
                        p.emit("vector", lambda e, a=av: e.tensor_add(x2t[:, :], x2t[:, :], a), inc=tV)
                        p.wait("sync", tV)
                        p.emit("sync", lambda e, d=x2d[row0:row0 + 128, :], s=x2t[:, :]:
                               e.dma_start(out=d, in_=s), inc=qS, k=16)
                    ln_stats_norm(av, sq2, fin2)

        if _DEBUG:
            dstg = aout[:, 0:4096]
            for nm_src, nm_dst, ncols in ((K_sb, dbg_k, KVH * S), (q_sb, dbg_q, H * NTOK),
                                          (V_sb, dbg_v, 16 * 1024), (attnF, dbg_af, H * 512)):
                for c0 in range(0, ncols, 4096):
                    w = min(4096, ncols - c0)
                    p.wait("vector", qS)
                    p.emit("vector", lambda e, d=dstg[:, 0:w], s=nm_src[:, c0:c0 + w]:
                           e.tensor_copy(out=d, in_=s), inc=tV)
                    p.wait("sync", tV)
                    p.emit("sync", lambda e, d=nm_dst[:, c0:c0 + w], s=dstg[:, 0:w]:
                           e.dma_start(out=d, in_=s), inc=qS, k=16)
            p.emit("sync", lambda e: e.dma_start(out=dbg_x2[:, :], in_=x2d[:, :]), inc=qS, k=16)
            p.wait("sync", qS)
        phase_barrier()
        ph2.close()
        attn_st.close()

        ph3 = contextlib.ExitStack()
        xn2 = SB(ph3, "xn2", [128, D], BF16)
        xn2F = SB(ph3, "xn2F", [128, 16 * 512], BF16)
        gated = SB(ph3, "gated", [128, 64 * 512], BF16)
        wgTp = [SB(ph3, "wgTa", [128, 16 * 512], BF16), SB(ph3, "wgTb", [128, 16 * 512], BF16)]
        wuTp = [SB(ph3, "wuTa", [128, 16 * 512], BF16), SB(ph3, "wuTb", [128, 16 * 512], BF16)]
        wdT = [SB(ph3, "wdT0", [128, 512], BF16), SB(ph3, "wdT1", [128, 512], BF16)]
        sg = SB(ph3, "sg", [128, 512], BF16)
        ffo = SB(ph3, "ffo", [128, 4 * D], F32)
        x3t = SB(ph3, "x3t", [128, D], F32)
        sq3 = SB(ph3, "sq3", [128, D], F32)
        ps_big = PS(ph3, "ps3_big", [128, S], F32)
        cell["ps_tr"] = PS(ph3, "ps3_tr", [128, 1024], BF16)
        cell["ps_tr2"] = PS(ph3, "ps3_tr2", [128, 1024], BF16)
        ps_g = PS(ph3, "ps3_g", [128, 512], F32)
        ps_u = PS(ph3, "ps3_u", [128, 512], F32)

        p.wait("sync", qT, stash_done)
        x2_stores = p.n(qS)
        p.wait("scalar", qS, x2_stores)
        for th in range(2):
            for tt in range(4):
                row0 = th * 512 + tt * 128
                p.wait("scalar", tV, ln_state["last_norm"].get("x3", 0))
                p.wait("scalar", qO)
                p.emit("scalar", lambda e, d=x3t[:, :], s=x2d[row0:row0 + 128, :]:
                       e.dma_start(out=d, in_=s), inc=qA, k=16)
                p.wait("vector", qA)
                p.wait("scalar", qA)

                def norm2():
                    p.emit("vector", lambda e: e.tensor_scalar(xn2[:, :], x3t[:, :], stat[:, 8:9],
                                                               stat[:, 20:21], ALU.subtract, ALU.mult), inc=tV)
                    ln_state["last_norm"]["x3"] = p.n(tV)
                ln_stats_norm(x3t[:, :], sq3[:, :], norm2)
                transpose_128(lambda i: xn2[:, i * 128:(i + 1) * 128], 16,
                              lambda i, tt=tt: xn2F[:, i * 512 + tt * 128: i * 512 + tt * 128 + 128], ident)

            gu_stamp = {0: 0, 1: 0}
            gu_ready = {0: 0, 1: 0}
            for hc in range(16):
                r0 = hc * 8192
                par = hc % 2
                wgT, wuT = wgTp[par], wuTp[par]
                p.wait("sync", tT, gu_stamp[par])
                p.emit("sync", lambda e, d=wgT[:, :], s=wgs[r0:r0 + 8192, :]:
                       e.dma_start(out=d, in_=s, transpose=True), inc=qS, k=16)
                p.emit("sync", lambda e, d=wuT[:, :], s=wus[r0:r0 + 8192, :]:
                       e.dma_start(out=d, in_=s, transpose=True), inc=qS, k=16)
                gu_ready[par] = p.n(qS)
                p.wait("tensor", qS, gu_ready[par])
                if _DEBUG and th == 0 and hc == 0:
                    for cc2 in range(4):
                        p.wait("vector", qS)
                        p.emit("vector", lambda e, cc2=cc2: e.tensor_copy(out=sq3[:, 0:2048],
                                                                          in_=wgT[:, cc2 * 2048:(cc2 + 1) * 2048]), inc=tV)
                        p.wait("sync", tV)
                        p.emit("sync", lambda e, cc2=cc2: e.dma_start(out=dbg_wgT[:, cc2 * 2048:(cc2 + 1) * 2048],
                                                                      in_=sq3[:, 0:2048]), inc=qS, k=16)
                    p.wait("vector", qS)
                    for cc2 in range(4):
                        p.emit("vector", lambda e, cc2=cc2: e.tensor_copy(out=sq3[:, 0:2048],
                                                                          in_=xn2F[:, cc2 * 2048:(cc2 + 1) * 2048]), inc=tV)
                        p.wait("sync", tV)
                        p.emit("sync", lambda e, cc2=cc2: e.dma_start(out=dbg_xnf2[:, cc2 * 2048:(cc2 + 1) * 2048],
                                                                      in_=sq3[:, 0:2048]), inc=qS, k=16)
                        p.wait("vector", qS)
                for hl in range(4):
                    proj_16(lambda dc, hl=hl, wgT=wgT: wgT[:, dc * 512 + hl * 128: dc * 512 + hl * 128 + 128],
                            lambda dc: xn2F[:, dc * 512:(dc + 1) * 512], ps_g, inc_last=False)
                    proj_16(lambda dc, hl=hl, wuT=wuT: wuT[:, dc * 512 + hl * 128: dc * 512 + hl * 128 + 128],
                            lambda dc: xn2F[:, dc * 512:(dc + 1) * 512], ps_u)
                    p.wait("scalar", tT)
                    p.wait("scalar", tV)
                    p.emit("scalar", lambda e: e.activation(sg[:, :], ps_g[:, :], AF.Silu), inc=tA)
                    p.wait("vector", tA)
                    hch = hc * 4 + hl
                    p.emit("vector", lambda e, d=gated[:, hch * 512:(hch + 1) * 512]:
                           e.tensor_mul(d, sg[:, :], ps_u[:, :]), inc=tV)
                    p.wait("tensor", tV)
                gu_stamp[par] = p.n(tT)
                if _DEBUG and th == 0 and hc == 1:
                    for cc2 in range(2):
                        p.emit("vector", lambda e, cc2=cc2: e.tensor_copy(out=sq3[:, 0:2048],
                                                                          in_=gated[:, cc2 * 2048:(cc2 + 1) * 2048]), inc=tV)
                        p.wait("sync", tV)
                        p.emit("sync", lambda e, cc2=cc2: e.dma_start(out=dbg_gated[:, cc2 * 2048:(cc2 + 1) * 2048],
                                                                      in_=sq3[:, 0:2048]), inc=qS, k=16)
                        p.wait("vector", qS)

            dn_n = {0: 0, 1: 0}
            dn_ready = {0: 0, 1: 0}
            for dh in range(2):
                for dq in range(2):
                    c0 = dh * 1024 + dq * 512

                    for hch in range(64):
                        wpar = hch % 2
                        wdt_ = wdT[wpar]
                        p.wait("sync", tT, dn_n[wpar])
                        p.emit("sync", lambda e, d=wdt_[:, :],
                               s=wdT_d[hch * 128:(hch + 1) * 128, c0:c0 + 512]:
                               e.dma_start(out=d, in_=s), inc=qS, k=16)
                        dn_ready[wpar] = p.n(qS)
                        p.wait("tensor", qS, dn_ready[wpar])
                        for tt in range(4):
                            lw = gated[:, hch * 512 + tt * 128: hch * 512 + tt * 128 + 128]
                            p.emit("tensor", lambda e, w=lw: e.ldweights(w))
                            p.emit("tensor", lambda e, o=ps_big[:, tt * 512:(tt + 1) * 512], w=lw,
                                   r2=wdt_[:, :], s=(hch == 0), t=(hch == 63):
                                   e.matmul(o, w, r2, start=s, stop=t), inc=(tT if tt == 3 else None))
                        dn_n[wpar] = p.n(tT)
                    p.wait("vector", tT)
                    p.wait("vector", qO)
                    for tt in range(4):
                        p.emit("vector", lambda e, s=ps_big[:, tt * 512:(tt + 1) * 512],
                               d=ffo[:, tt * D + c0: tt * D + c0 + 512]:
                               e.tensor_copy(out=d, in_=s), inc=(tV if tt == 3 else None))
                    p.wait("tensor", tV)

            if _DEBUG and th == 0:
                for cc2 in range(4):
                    p.wait("sync", tV)
                    p.emit("sync", lambda e, cc2=cc2: e.dma_start(out=dbg_ffo[:, cc2 * D:(cc2 + 1) * D],
                                                                  in_=ffo[:, cc2 * D:(cc2 + 1) * D]), inc=qS, k=16)
                p.wait("scalar", qS)
                p.wait("vector", qS)
            for tt in range(4):
                row0 = th * 512 + tt * 128
                fv = ffo[:, tt * D:(tt + 1) * D]
                p.wait("scalar", qO)
                p.emit("scalar", lambda e, d=x3t[:, :], s=x2d[row0:row0 + 128, :]:
                       e.dma_start(out=d, in_=s), inc=qA, k=16)

                def fin3(fv=fv, row0=row0):
                    p.emit("vector", lambda e, a=fv: e.tensor_scalar(a, a, stat[:, 8:9], stat[:, 20:21],
                                                                     ALU.subtract, ALU.mult), inc=tV)
                    p.wait("vector", tV)
                    p.wait("vector", qA)
                    p.wait("vector", qO)
                    p.emit("vector", lambda e, a=fv: e.tensor_add(sq3[:, :], x3t[:, :], a), inc=tV)
                    p.wait("sync", tV)
                    p.emit("sync", lambda e, d=y[row0:row0 + 128, :], s=sq3[:, :]:
                           e.dma_start(out=d, in_=s), inc=qO, k=16)
                ln_stats_norm(fv, sq3[:, :], fin3)

        ph3.close()
        p.wait("sync", qO, p.n(qO))

        p.realize(root.enter_context(nc.Block()))
    return nc


def _shard_inputs(inputs):
    t = _host_tables()
    x = np.ascontiguousarray(np.asarray(inputs["L0_attn_x"], dtype=np.float32))
    per_core = []
    shared = {
        "Wq": np.ascontiguousarray(np.asarray(inputs["L0_attn_Wq"], np.float32)),
        "Wkv": np.ascontiguousarray(np.asarray(inputs["L0_attn_Wkv"], np.float32)),
        "Wo": np.ascontiguousarray(np.asarray(inputs["L0_attn_Wo"], np.float32)),
        "bo": np.ascontiguousarray(np.asarray(inputs["L0_attn_bo"], np.float32).reshape(1, D)),
        "wg": np.ascontiguousarray(np.asarray(inputs["L0_ffn_w_gate"], np.float32)),
        "wu": np.ascontiguousarray(np.asarray(inputs["L0_ffn_w_up"], np.float32)),
        "wd": np.ascontiguousarray(np.asarray(inputs["L0_ffn_w_down"], np.float32)),
        "T_A": t["T_A"],
        "identin": t["ident16"], "ident32in": t["ident32"],
        "onesrow": t["ones_row"], "onescol": t["ones_col16"], "sq128": t["sqrt128"],
    }
    for c in range(8):
        b, hh = c // 2, c % 2
        roll = np.concatenate([np.arange(hh * NTOK, (hh + 1) * NTOK),
                               np.arange((1 - hh) * NTOK, (2 - hh) * NTOK)])
        em = np.ones((128, 8), np.float32)
        kc = np.ones((128, 8), np.float32)
        kc2 = np.ones((128, 1), np.float32)
        if hh == 0:
            em[0:64, 0] = 0.0
            kc[:, :] = 0.0
            kc[0:64, 0] = 1.0
            kc2[:, :] = 1.0
        else:
            kc[:, :] = 1.0
            kc2[:, :] = 0.0
        m = dict(shared)
        m.update({
            "x_own": np.ascontiguousarray(x[b, hh * NTOK:(hh + 1) * NTOK, :]),
            "x_oth": np.ascontiguousarray(x[b, (1 - hh) * NTOK:(2 - hh) * NTOK, :]),
            "T_X": (t["T_A"] if hh == 0 else t["T_B"]),
            "sinq_hi": np.ascontiguousarray(t["sin_hi"][:, hh * NTOK:(hh + 1) * NTOK]),
            "sinq_lo": np.ascontiguousarray(t["sin_lo"][:, hh * NTOK:(hh + 1) * NTOK]),
            "cosq_hi": np.ascontiguousarray(t["cos_hi"][:, hh * NTOK:(hh + 1) * NTOK]),
            "cosq_lo": np.ascontiguousarray(t["cos_lo"][:, hh * NTOK:(hh + 1) * NTOK]),
            "sink_hi": np.ascontiguousarray(t["sin_hi"][:, roll]),
            "sink_lo": np.ascontiguousarray(t["sin_lo"][:, roll]),
            "cosk_hi": np.ascontiguousarray(t["cos_hi"][:, roll]),
            "cosk_lo": np.ascontiguousarray(t["cos_lo"][:, roll]),
            "expmask": em, "keepcol": kc, "keepcol2": kc2,
        })
        per_core.append(m)
    return per_core


def kernel(**inputs):
    per_core = _shard_inputs(inputs)
    nc = build_program()
    res = run_bass_kernel_spmd(nc, per_core, core_ids=list(range(8)))
    out = np.empty((B, S, D), np.float32)
    for c in range(8):
        b, hh = c // 2, c % 2
        out[b, hh * NTOK:(hh + 1) * NTOK, :] = res.results[c]["y"]
    return out


# version 10 (final)
# speedup vs baseline: 1.1357x; 1.0367x over previous
import contextlib
import numpy as np
import ml_dtypes
import concourse.bass as bass
import concourse.mybir as mybir
from concourse.bass_utils import run_bass_kernel_spmd

F32 = mybir.dt.float32
BF16 = mybir.dt.bfloat16
AF = mybir.ActivationFunctionType
ALU = mybir.AluOpType
AX = mybir.AxisListType

B, S, D = 4, 2048, 2048
H, KVH, HD, HALF = 16, 8, 128, 64
DH = 8192
NTOK = 1024
NEG = np.float32(-1.0e9)

_cache = {}
_DEBUG = False


def _host_tables():
    if "t" in _cache:
        return _cache["t"]
    base32 = np.float32(2.0) ** (-(np.float32(2.0) ** np.float32(-1.0)))
    slopes32 = (np.float64(base32) ** np.arange(1, 17, dtype=np.float64)).astype(np.float32)
    au = np.abs(np.arange(-2047, 2048))
    E0 = (slopes32[:, None] * (-au[None, :]).astype(np.float32)).astype(np.float32)
    E0[:, au <= 256] = NEG
    Epad = np.concatenate([np.zeros((16, 127), np.float32), E0], axis=1)
    idx = np.clip((np.arange(4095)[None, :] - np.arange(128)[:, None]) + 127, 0, Epad.shape[1] - 1)
    T_A = np.ascontiguousarray(Epad[:, idx])
    idx2 = (np.arange(4095)[None, :] - np.arange(128)[:, None]) - 2048 + 127
    ok = (idx2 >= 127) & (idx2 < Epad.shape[1])
    T_B = np.ascontiguousarray(Epad[:, np.clip(idx2, 0, Epad.shape[1] - 1)] *
                               ok[None, :, :].astype(np.float32))

    freqs = 1.0 / (10000.0 ** (np.arange(HALF, dtype=np.float64) / HALF))
    ph = freqs[:, None] * np.arange(S, dtype=np.float64)[None, :]

    def hilo(v):
        hi = v.astype(np.float32)
        lo = (v - hi.astype(np.float64)).astype(np.float32)
        return np.ascontiguousarray(hi), np.ascontiguousarray(lo)
    sin_hi, sin_lo = hilo(np.sin(ph))
    cos_hi, cos_lo = hilo(np.cos(ph))
    _cache["t"] = dict(
        T_A=T_A, T_B=T_B, sin_hi=sin_hi, sin_lo=sin_lo, cos_hi=cos_hi, cos_lo=cos_lo,
        ident16=np.eye(128, dtype=ml_dtypes.bfloat16),
        ident32=np.eye(128, dtype=np.float32),
        ones_row=np.ones((1, 128), np.float32),
        ones_col16=np.ones((128, 1), ml_dtypes.bfloat16),
        sqrt128=np.sqrt(np.float32(128.0)).astype(np.float32).reshape(1, 1),
    )
    return _cache["t"]


class P:
    ENGS = ("sync", "tensor", "vector", "scalar", "gpsimd")

    def __init__(self, nc, es):
        self.nc = nc
        self.q = {e: [] for e in self.ENGS}
        self.sems = {}
        for name in ("qS", "qA", "qG", "qT", "tT", "tV", "tA", "tG", "qO"):
            h = es.enter_context(nc.semaphore(name))
            self.sems[name] = [h, 0]
        self.issuer = {"qS": "sync", "qA": "scalar", "qG": "gpsimd", "qT": "gpsimd", "qO": "sync"}
        self.boundaries = {k: set() for k in self.issuer}

    def n(self, name):
        return self.sems[name][1]

    def emit(self, eng, fn, inc=None, k=1):
        if inc is None:
            self.q[eng].append(fn)
            return None
        s = self.sems[inc]
        s[1] += k
        self.q[eng].append(lambda e, fn=fn, h=s[0], k=k: fn(e).then_inc(h, k))
        return s[1]

    def wait(self, eng, name, val=None):
        s = self.sems[name]
        v = s[1] if val is None else val
        if v <= 0:
            return
        iss = self.issuer.get(name)
        if iss is not None and eng != iss and v not in self.boundaries[name]:
            self.boundaries[name].add(v)
            self.q[iss].append(lambda e, h=s[0], v=v: e.wait_ge(h, v))
        self.q[eng].append(lambda e, h=s[0], v=v: e.wait_ge(h, v))

    def realize(self, block):
        for eng in self.ENGS:
            ops = self.q[eng]

            def runner(engobj, ops=ops):
                for f in ops:
                    f(engobj)
            getattr(block, eng)(runner)


def build_program():
    nc = bass.Bass("TRN2", target_bir_lowering=False, debug=False)
    dt = nc.dram_tensor

    x_own = dt("x_own", [NTOK, D], F32, kind="ExternalInput").ap()
    x_oth = dt("x_oth", [NTOK, D], F32, kind="ExternalInput").ap()
    Wq = dt("Wq", [D, D], F32, kind="ExternalInput").ap()
    Wkv = dt("Wkv", [D, D], F32, kind="ExternalInput").ap()
    Wo = dt("Wo", [D, D], F32, kind="ExternalInput").ap()
    bo = dt("bo", [1, D], F32, kind="ExternalInput").ap()
    wg = dt("wg", [DH, D], F32, kind="ExternalInput").ap()
    wu = dt("wu", [DH, D], F32, kind="ExternalInput").ap()
    wd = dt("wd", [D, DH], F32, kind="ExternalInput").ap()
    T_A = dt("T_A", [16, 128, 4095], F32, kind="ExternalInput").ap()
    T_X = dt("T_X", [16, 128, 4095], F32, kind="ExternalInput").ap()
    tabs_in = {}
    for nm in ("sinq_hi", "sinq_lo", "cosq_hi", "cosq_lo"):
        tabs_in[nm] = dt(nm, [HALF, NTOK], F32, kind="ExternalInput").ap()
    for nm in ("sink_hi", "sink_lo", "cosk_hi", "cosk_lo"):
        tabs_in[nm] = dt(nm, [HALF, S], F32, kind="ExternalInput").ap()
    identin = dt("identin", [128, 128], BF16, kind="ExternalInput").ap()
    ident32in = dt("ident32in", [128, 128], F32, kind="ExternalInput").ap()
    onesrow = dt("onesrow", [1, 128], F32, kind="ExternalInput").ap()
    onescol = dt("onescol", [128, 1], BF16, kind="ExternalInput").ap()
    sq128 = dt("sq128", [1, 1], F32, kind="ExternalInput").ap()
    expmask = dt("expmask", [128, 8], F32, kind="ExternalInput").ap()
    keepcol = dt("keepcol", [128, 8], F32, kind="ExternalInput").ap()
    keepcol2 = dt("keepcol2", [128, 1], F32, kind="ExternalInput").ap()
    y = dt("y", [NTOK, D], F32, kind="ExternalOutput").ap()
    wgs = dt("wgs", [16 * 16 * 512, 128], BF16, kind="Internal").ap()
    wus = dt("wus", [16 * 16 * 512, 128], BF16, kind="Internal").ap()
    wds = dt("wds", [D, DH], BF16, kind="Internal").ap()
    wdT_d = dt("wdT_d", [DH, D], BF16, kind="Internal").ap()
    x2d = dt("x2d", [NTOK, D], F32, kind="Internal").ap()
    if _DEBUG:
        dbg_x2 = dt("dbg_x2", [NTOK, D], F32, kind="ExternalOutput").ap()
        dbg_xnf = dt("dbg_xnf", [128, 16 * 512], F32, kind="ExternalOutput").ap()
        dbg_xt = dt("dbg_xt", [128, D], F32, kind="ExternalOutput").ap()
        dbg_stat = dt("dbg_stat", [128, 24], F32, kind="ExternalOutput").ap()
        dbg_st4 = dt("dbg_st4", [128, 96], F32, kind="ExternalOutput").ap()
        dbg_xn = dt("dbg_xn", [128, D], F32, kind="ExternalOutput").ap()
        dbg_kps = dt("dbg_kps", [128, 512], F32, kind="ExternalOutput").ap()
        dbg_tab = dt("dbg_tab", [64, 4 * S], F32, kind="ExternalOutput").ap()
        dbg_wgT = dt("dbg_wgT", [128, 16 * 512], F32, kind="ExternalOutput").ap()
        dbg_xnf2 = dt("dbg_xnf2", [128, 16 * 512], F32, kind="ExternalOutput").ap()
        dbg_gated = dt("dbg_gated", [128, 8 * 512], F32, kind="ExternalOutput").ap()
        dbg_ffo = dt("dbg_ffo", [128, 4 * D], F32, kind="ExternalOutput").ap()
        dbg_k = dt("dbg_k", [128, KVH * S], F32, kind="ExternalOutput").ap()
        dbg_q = dt("dbg_q", [128, H * NTOK], F32, kind="ExternalOutput").ap()
        dbg_v = dt("dbg_v", [128, 16 * 1024], F32, kind="ExternalOutput").ap()
        dbg_af = dt("dbg_af", [128, H * 512], F32, kind="ExternalOutput").ap()

    root = contextlib.ExitStack()
    with root:
        def SB(st, name, shape, dty):
            return st.enter_context(nc.sbuf_tensor(name, shape, dty))

        def PS(st, name, shape, dty=F32):
            return st.enter_context(nc.psum_tensor(name, shape, dty))

        p = P(nc, root)
        qS, qA, qG, tT, tV, tA, tG, qO = "qS", "qA", "qG", "tT", "tV", "tA", "tG", "qO"

        ident = SB(root, "ident", [128, 128], BF16)
        ident32 = SB(root, "ident32", [128, 128], F32)
        ones_r = SB(root, "ones_r", [1, 128], F32)
        ones_c = SB(root, "ones_c", [128, 1], BF16)
        rmask = SB(root, "rmask", [128, 8], F32)
        kcol = SB(root, "kcol", [128, 8], F32)
        kcol2 = SB(root, "kcol2", [128, 1], F32)
        eps_t = SB(root, "eps_t", [128, 1], F32)
        sc_rec = SB(root, "sc_rec", [1, 1], F32)
        stat = SB(root, "stat", [128, 24], F32)
        attn_st = contextlib.ExitStack()
        K_sb = SB(attn_st, "K_sb", [128, KVH * S], BF16)
        V_sb = SB(attn_st, "V_sb", [128, 16 * 1024], BF16)
        q_sb = SB(attn_st, "q_sb", [128, H * NTOK], BF16)
        cosC = SB(attn_st, "cosC", [HALF, NTOK], BF16)
        sinC = SB(attn_st, "sinC", [HALF, NTOK], BF16)

        cell = {}

        def phase_barrier():
            for eng in ("sync", "gpsimd", "scalar", "vector", "tensor"):
                for sem in (tT, tV, tA, tG):
                    p.wait(eng, sem)

        def load_wcols(w_ap, c0, ncols):
            wt = cell["wt"]
            par = wt["par"]
            wt["par"] ^= 1
            tile = wt["tiles"][par]
            p.wait("gpsimd", tT, wt["stamp"][par])
            src = w_ap[:, c0:c0 + ncols].rearrange("(dc q) c -> q dc c", q=128)
            dstv = tile[:, 0:16 * ncols].rearrange("q (dc c) -> q dc c", c=ncols)
            p.emit("gpsimd", lambda e, a=dstv, b=src: e.dma_start(out=a, in_=b), inc=qG, k=16)
            wt["ready"][par] = p.n(qG)
            wt["cur"] = par
            return tile

        def wt_ready():
            wt = cell["wt"]
            p.wait("tensor", qG, wt["ready"][wt["cur"]])

        def wt_used():
            wt = cell["wt"]
            wt["stamp"][wt["cur"]] = p.n(tT)

        def proj_16(lhs_fn, rhs_fn, pst, inc_last=True):
            if not isinstance(pst, bass.AP):
                pst = pst[:, :]
            for dc in range(16):
                w = lhs_fn(dc)
                p.emit("tensor", lambda e, w=w: e.ldweights(w))
                p.emit("tensor", lambda e, o=pst, w=w, r=rhs_fn(dc), s=(dc == 0), t=(dc == 15):
                       e.matmul(o, w, r, start=s, stop=t),
                       inc=(tT if (dc == 15 and inc_last) else None))

        def transpose_128(src_fn, n_t, dst_fn, idmat):
            if not isinstance(idmat, bass.AP):
                idmat = idmat[:, :]
            p.wait("tensor", tV)
            ptr, ptr2 = cell["ps_tr"], cell["ps_tr2"]
            done = 0
            while done < n_t:
                grp = min(8, n_t - done)
                pst = ptr if (done // 8) % 2 == 0 else ptr2
                for i in range(grp):
                    p.emit("tensor", lambda e, s=src_fn(done + i), d=pst[:, i * 128:(i + 1) * 128]:
                           e.transpose(d, s, idmat), inc=(tT if i == grp - 1 else None))
                p.wait("vector", tT)
                for i in range(grp):
                    p.emit("vector", lambda e, s=pst[:, i * 128:(i + 1) * 128], d=dst_fn(done + i):
                           e.tensor_copy(out=d, in_=s), inc=(tV if i == grp - 1 else None))
                p.wait("tensor", tV)
                done += grp

        ln_state = {"last_sq_red": 0, "last_norm": {}}

        def ln_stats_norm(src_tile, sqbuf, out_fn):
            p.wait("vector", tV)
            p.emit("vector", lambda e, a=src_tile: e.reduce_sum(stat[:, 0:1], a, axis=AX.X), inc=tV)
            p.wait("scalar", tV)
            p.wait("scalar", qO)
            p.emit("scalar", lambda e, a=src_tile, q=sqbuf: e.activation(q, a, AF.Square), inc=tA)
            p.wait("vector", tA)
            p.emit("vector", lambda e, q=sqbuf: e.reduce_sum(stat[:, 16:17], q, axis=AX.X), inc=tV)
            ln_state["last_sq_red"] = p.n(tV)
            p.wait("vector", tV)
            p.emit("vector", lambda e: e.tensor_scalar(stat[:, 8:9], stat[:, 0:1], 1.0 / D, None, ALU.mult), inc=tV)
            p.wait("vector", tV)
            p.emit("vector", lambda e: e.tensor_mul(stat[:, 12:13], stat[:, 8:9], stat[:, 8:9]), inc=tV)
            p.wait("vector", tV)
            p.emit("vector", lambda e: e.scalar_tensor_tensor(stat[:, 16:17], stat[:, 16:17], 1.0 / D,
                                                              stat[:, 12:13], ALU.mult, ALU.subtract), inc=tV)
            p.wait("scalar", tV)
            p.emit("scalar", lambda e: e.activation(stat[:, 20:21], stat[:, 16:17], AF.Ln, bias=eps_t[:]), inc=tA)
            p.wait("scalar", tA)
            p.emit("scalar", lambda e: e.activation(stat[:, 20:21], stat[:, 20:21], AF.Exp, scale=-0.5), inc=tA)
            p.wait("vector", tA)
            p.wait("vector", tV)
            out_fn()

        ph1 = contextlib.ExitStack()
        cosk = SB(ph1, "cosk", [HALF, S], BF16)
        sink = SB(ph1, "sink", [HALF, S], BF16)
        cosq = SB(ph1, "cosq", [HALF, NTOK], BF16)
        sinq = SB(ph1, "sinq", [HALF, NTOK], BF16)
        c_row = SB(ph1, "c_row", [1, NTOK], F32)
        x_t = [SB(ph1, "x_t0", [128, D], F32), SB(ph1, "x_t1", [128, D], F32)]
        xn1 = SB(ph1, "xn1", [128, D], BF16)
        xn1F = SB(ph1, "xn1F", [128, 16 * 512], BF16)
        cell["wt"] = {"tiles": [SB(ph1, "wtile1a", [128, 16 * 512], BF16),
                                SB(ph1, "wtile1b", [128, 16 * 512], BF16)],
                      "par": 0, "stamp": [0, 0], "ready": [0, 0], "cur": 0}
        sq_s = SB(ph1, "sq_s", [128, D], F32)
        rt = [SB(ph1, f"rt{i}", [HALF, 512], F32) for i in range(4)]
        csb = SB(ph1, "csb", [HALF, 512], F32)
        c_col = SB(ph1, "c_col", [128, 4], F32)
        hi_sb = x_t[0][0:HALF, 0:S]
        lo_sb = x_t[1][0:HALF, 0:S]
        ps_main = PS(ph1, "ps1_main", [128, S], F32)
        cell["ps_tr"] = PS(ph1, "ps1_tr", [128, 1024], BF16)
        cell["ps_tr2"] = PS(ph1, "ps1_tr2", [128, 1024], BF16)
        ps_aux = PS(ph1, "ps1_aux", [128, 512], F32)
        ps_p = [ps_main[:, 0:512], ps_main[:, 512:1024]]

        wgv = wg.rearrange("(hc hl) (dc q) -> hc dc hl q", hl=512, q=128)
        wuv = wu.rearrange("(hc hl) (dc q) -> hc dc hl q", hl=512, q=128)
        wgsv = wgs.rearrange("(hc dc hl) q -> hc dc hl q", dc=16, hl=512)
        wusv = wus.rearrange("(hc dc hl) q -> hc dc hl q", dc=16, hl=512)
        qT = "qT"
        for hc in range(16):
            p.emit("gpsimd", lambda e, a=wgsv[hc], b=wgv[hc]:
                   e.dma_start(out=a, in_=b), inc=qT, k=16)
            p.emit("gpsimd", lambda e, a=wusv[hc], b=wuv[hc]:
                   e.dma_start(out=a, in_=b), inc=qT, k=16)
        for dc in range(8):
            r0 = dc * 256
            p.emit("gpsimd", lambda e, a=wds[r0:r0 + 256, :], b=wd[r0:r0 + 256, :]:
                   e.dma_start(out=a, in_=b), inc=qT, k=16)
        stash_done = p.n(qT)

        for dst, hin, lin, n in ((cosk, "cosk_hi", "cosk_lo", S), (sink, "sink_hi", "sink_lo", S),
                                 (cosq, "cosq_hi", "cosq_lo", NTOK), (sinq, "sinq_hi", "sinq_lo", NTOK)):
            p.emit("sync", lambda e, a=hi_sb[:, 0:n], b=tabs_in[hin]: e.dma_start(out=a, in_=b), inc=qS, k=16)
            p.emit("sync", lambda e, a=lo_sb[:, 0:n], b=tabs_in[lin]: e.dma_start(out=a, in_=b), inc=qS, k=16)
            p.wait("vector", qS)
            p.emit("vector", lambda e, d=dst[:, 0:n], a=hi_sb[:, 0:n], b=lo_sb[:, 0:n]:
                   e.tensor_add(d, a, b), inc=tV)
            p.wait("sync", tV)
        for dsb, din in ((ident, identin), (ident32, ident32in), (ones_r, onesrow),
                         (ones_c, onescol), (sc_rec, sq128), (rmask, expmask),
                         (kcol, keepcol), (kcol2, keepcol2)):
            p.emit("sync", lambda e, a=dsb[:, :], b=din: e.dma_start(out=a, in_=b), inc=qS, k=16)
        p.wait("vector", qS)
        p.emit("vector", lambda e: e.memset(eps_t[:], 1e-5), inc=tV)
        p.emit("vector", lambda e: e.reciprocal(sc_rec[:], sc_rec[:]), inc=tV)
        p.wait("tensor", qS)
        p.wait("scalar", qS)
        p.wait("gpsimd", qS)
        p.wait("sync", tV)
        wdstg = [SB(ph1, "wdstg0", [128, 1024], BF16), SB(ph1, "wdstg1", [128, 1024], BF16)]
        wd_tr_n = {0: 0, 1: 0}
        p.wait("sync", qT, stash_done)
        for hch in range(64):
            for dhf in range(2):
                par = (hch * 2 + dhf) % 2
                p.wait("sync", qS, wd_tr_n[par])
                p.emit("sync", lambda e, d=wdstg[par][:, :],
                       s=wds[dhf * 1024:(dhf + 1) * 1024, hch * 128:(hch + 1) * 128]:
                       e.dma_start(out=d, in_=s, transpose=True), inc=qS, k=16)
                p.wait("sync", qS)
                p.emit("sync", lambda e, d=wdT_d[hch * 128:(hch + 1) * 128, dhf * 1024:(dhf + 1) * 1024],
                       s=wdstg[par][:, :]: e.dma_start(out=d, in_=s), inc=qS, k=16)
                wd_tr_n[par] = p.n(qS)
        wd_tr_done = p.n(qS)

        for sp in range(4):
            src = x_own if sp < 2 else x_oth
            base = (sp % 2) * 512
            for tt in range(4):
                par = tt % 2
                xt_ = x_t[par]
                p.wait("scalar", tV, ln_state["last_norm"].get(par, 0))
                p.emit("scalar", lambda e, a=xt_[:, :], b=src[base + tt * 128: base + (tt + 1) * 128, :]:
                       e.dma_start(out=a, in_=b), inc=qA, k=16)
                p.wait("vector", qA)
                p.wait("scalar", qA)

                def norm_out(xt_=xt_, sp=sp, tt=tt):
                    p.emit("vector", lambda e, a=xt_[:, :]:
                           e.tensor_scalar(xn1[:, :], a, stat[:, 8:9], stat[:, 20:21],
                                           ALU.subtract, ALU.mult), inc=tV)
                    ln_state["last_norm"][tt % 2] = p.n(tV)
                    if sp < 2:
                        p.wait("vector", tV)
                        p.emit("vector", lambda e, cc=c_col[:, tt:tt + 1]:
                               e.reduce_sum(cc, xn1[:, :], axis=AX.X, apply_absolute_value=True), inc=tV)
                ln_stats_norm(xt_[:, :], sq_s[:, :], norm_out)
                if _DEBUG and sp == 0:
                    p.emit("vector", lambda e, d_=dbg_st4, tt=tt:
                           e.tensor_copy(out=sq_s[:, 1024 + tt * 24:1024 + (tt + 1) * 24], in_=stat[:, :]), inc=tV)
                    if tt == 0:
                        p.emit("vector", lambda e: e.tensor_copy(out=sq_s[:, 256:256 + D - 256],
                                                                 in_=xn1[:, 0:D - 256]), inc=tV)
                        p.wait("sync", tV)
                        p.emit("sync", lambda e: e.dma_start(out=dbg_xn[:, 0:D - 256],
                                                             in_=sq_s[:, 256:256 + D - 256]), inc=qS, k=16)
                        p.wait("vector", qS)
                if _DEBUG and sp == 0 and tt == 3:
                    p.wait("sync", tV)
                    p.emit("sync", lambda e: e.dma_start(out=dbg_st4[:, :], in_=sq_s[:, 1024:1024 + 96]), inc=qS, k=16)
                    p.wait("vector", qS)
                if _DEBUG and sp == 0 and tt == 0:
                    p.wait("sync", tV)
                    p.emit("sync", lambda e, d=dbg_xt[:, :], s=xt_[:, :]: e.dma_start(out=d, in_=s), inc=qS, k=16)
                    p.emit("sync", lambda e: e.dma_start(out=dbg_stat[:, :], in_=stat[:, :]), inc=qS, k=16)
                    p.wait("vector", qS)
                transpose_128(lambda i: xn1[:, i * 128:(i + 1) * 128], 16,
                              lambda i, tt=tt: xn1F[:, i * 512 + tt * 128: i * 512 + tt * 128 + 128], ident)
            xn1F_ready = p.n(tV)
            if _DEBUG and sp == 0:
                for cc in range(4):
                    p.emit("vector", lambda e, d=sq_s[:, :], s=xn1F[:, cc * 2048:(cc + 1) * 2048]:
                           e.tensor_copy(out=d, in_=s), inc=tV)
                    p.wait("sync", tV)
                    p.emit("sync", lambda e, d=dbg_xnf[:, cc * 2048:(cc + 1) * 2048], s=sq_s[:, :]:
                           e.dma_start(out=d, in_=s), inc=qS, k=16)
                    p.wait("vector", qS)

            if sp < 2:
                p.wait("scalar", tV, xn1F_ready)
                p.emit("scalar", lambda e: e.activation(c_col[:, 0:4], c_col[:, 0:4], AF.Sigmoid), inc=tA)
                p.wait("tensor", tA)
                for tt in range(4):
                    p.emit("tensor", lambda e, s=c_col[:, tt:tt + 1]:
                           e.transpose(ps_aux[0:1, 0:128], s, ident32[:, :]), inc=tT)
                    p.wait("vector", tT)
                    p.emit("vector", lambda e, d=c_row[0:1, sp * 512 + tt * 128: sp * 512 + (tt + 1) * 128]:
                           e.tensor_copy(out=d, in_=ps_aux[0:1, 0:128]), inc=tV)
                    p.wait("tensor", tV)
                p.wait("vector", tV)
                p.emit("vector", lambda e, a=c_row[0:1, sp * 512:(sp + 1) * 512]:
                       e.tensor_scalar(a, a, sc_rec[0:1, 0:1], None, ALU.mult), inc=tV)
                p.wait("tensor", tV)
                p.emit("tensor", lambda e, r=c_row[0:1, sp * 512:(sp + 1) * 512]:
                       e.matmul(ps_aux[0:64, :], ones_r[0:1, 0:64], r, start=True, stop=True), inc=tT)
                p.wait("vector", tT)
                p.emit("vector", lambda e: e.tensor_copy(out=csb[:, :], in_=ps_aux[0:64, :]), inc=tV)
                p.wait("tensor", tV)
                p.wait("vector", tV)
                p.emit("vector", lambda e, d=cosC[:, sp * 512:(sp + 1) * 512],
                       t=cosq[:, sp * 512:(sp + 1) * 512]: e.tensor_mul(d, t, csb[:, :]), inc=tV)
                p.emit("vector", lambda e, d=sinC[:, sp * 512:(sp + 1) * 512],
                       t=sinq[:, sp * 512:(sp + 1) * 512]: e.tensor_mul(d, t, csb[:, :]), inc=tV)

            def rope_evac(pst, tabc, tabs, dst_lo, dst_hi):
                p.wait("vector", tV)
                p.emit("vector", lambda e, a=pst[0:64, :], c=tabc: e.tensor_mul(rt[0][:, :], a, c), inc=tV)
                p.emit("vector", lambda e, a=pst[64:128, :], s2=tabs: e.tensor_mul(rt[1][:, :], a, s2), inc=tV)
                p.emit("vector", lambda e, a=pst[0:64, :], s2=tabs: e.tensor_mul(rt[2][:, :], a, s2), inc=tV)
                p.emit("vector", lambda e, a=pst[64:128, :], c=tabc: e.tensor_mul(rt[3][:, :], a, c), inc=tV)
                p.wait("vector", tV)
                p.emit("vector", lambda e, d=dst_lo: e.tensor_sub(d, rt[0][:, :], rt[1][:, :]), inc=tV)
                p.emit("vector", lambda e, d=dst_hi: e.tensor_add(d, rt[2][:, :], rt[3][:, :]), inc=tV)

            for ocq in range(4):
                wtl = load_wcols(Wkv, ocq * 256, 256)
                wt_ready()
                p.wait("tensor", tV, xn1F_ready)
                for oc2 in range(2):
                    oc = ocq * 2 + oc2
                    proj_16(lambda dc, oc2=oc2, wt=wtl: wt[:, dc * 256 + oc2 * 128: dc * 256 + oc2 * 128 + 128],
                            lambda dc: xn1F[:, dc * 512:(dc + 1) * 512], ps_p[oc2])
                    p.wait("vector", tT)
                    if _DEBUG and sp == 0 and ocq == 0 and oc2 == 0:
                        p.emit("vector", lambda e: e.tensor_copy(out=sq_s[:, 0:512], in_=ps_p[0]), inc=tV)
                        p.wait("sync", tV)
                        p.emit("sync", lambda e: e.dma_start(out=dbg_kps[:, :], in_=sq_s[:, 0:512]), inc=qS, k=16)
                        p.wait("vector", qS)
                    rope_evac(ps_p[oc2], cosk[:, sp * 512:(sp + 1) * 512], sink[:, sp * 512:(sp + 1) * 512],
                              K_sb[0:64, oc * S + sp * 512: oc * S + (sp + 1) * 512],
                              K_sb[64:128, oc * S + sp * 512: oc * S + (sp + 1) * 512])
                wt_used()
                p.wait("tensor", tV)

            for vq in range(2):
                wtl = load_wcols(Wkv, 1024 + vq * 512, 512)
                wt_ready()
                for tt in range(4):
                    if tt >= 2:
                        p.wait("tensor", tV)
                    proj_16(lambda dc, tt=tt: xn1F[:, dc * 512 + tt * 128: dc * 512 + tt * 128 + 128],
                            lambda dc, wt=wtl: wt[:, dc * 512:(dc + 1) * 512], ps_p[tt % 2])
                    p.wait("vector", tT)
                    kt = sp * 4 + tt
                    p.emit("vector", lambda e, s=ps_p[tt % 2],
                           d=V_sb[:, kt * 1024 + vq * 512: kt * 1024 + (vq + 1) * 512]:
                           e.tensor_copy(out=d, in_=s), inc=tV)
                wt_used()
                p.wait("tensor", tV)

            if sp < 2:
                for ocq in range(4):
                    wtl = load_wcols(Wq, ocq * 512, 512)
                    wt_ready()
                    for oc2 in range(4):
                        oc = ocq * 4 + oc2
                        if oc2 >= 2:
                            p.wait("tensor", tV)
                        proj_16(lambda dc, oc2=oc2, wt=wtl:
                                wt[:, dc * 512 + oc2 * 128: dc * 512 + oc2 * 128 + 128],
                                lambda dc: xn1F[:, dc * 512:(dc + 1) * 512], ps_p[oc2 % 2])
                        p.wait("vector", tT)
                        rope_evac(ps_p[oc2 % 2], cosC[:, sp * 512:(sp + 1) * 512],
                                  sinC[:, sp * 512:(sp + 1) * 512],
                                  q_sb[0:64, oc * NTOK + sp * 512: oc * NTOK + (sp + 1) * 512],
                                  q_sb[64:128, oc * NTOK + sp * 512: oc * NTOK + (sp + 1) * 512])
                    wt_used()
                    p.wait("tensor", tV)

        if _DEBUG:
            for ii, tb in enumerate((cosk, sink, cosq, sinq)):
                n2 = S if ii < 2 else NTOK
                p.emit("vector", lambda e, tb=tb, n2=n2: e.tensor_copy(out=hi_sb[:, 0:n2], in_=tb[:, 0:n2]), inc=tV)
                p.wait("sync", tV)
                p.emit("sync", lambda e, ii=ii, n2=n2: e.dma_start(out=dbg_tab[:, ii * S: ii * S + n2],
                                                                   in_=hi_sb[:, 0:n2]), inc=qS, k=16)
                p.wait("vector", qS)
        phase_barrier()
        ph1.close()

        ph2 = contextlib.ExitStack()
        s_sb = [SB(ph2, "s_sb0", [128, S], F32), SB(ph2, "s_sb1", [128, S], F32)]
        p_sb = [SB(ph2, "p_sb0", [128, S], BF16), SB(ph2, "p_sb1", [128, S], BF16)]
        a_single = SB(ph2, "a_t0", [128, S], F32)
        a_t = [a_single, a_single]
        pT_sb = SB(ph2, "pT_sb", [128, 16 * 512], BF16)
        attnF = SB(ph2, "attnF", [128, H * 512], BF16)
        rinvb = SB(ph2, "rinvb", [128, 512], F32)
        rsum = rinvb[0:1, :]
        nmax = SB(ph2, "nmax", [128, 8], F32)
        ebias = SB(ph2, "ebias", [128, 8], F32)
        cell["wt"] = {"tiles": [SB(ph2, "wtile2a", [128, 16 * 128], BF16),
                                SB(ph2, "wtile2b", [128, 16 * 128], BF16)],
                      "par": 0, "stamp": [0, 0], "ready": [0, 0], "cur": 0}
        aout = SB(ph2, "aout", [128, 2 * D], F32)
        x2t = SB(ph2, "x2t", [128, D], F32)
        bo_b = SB(ph2, "bo_b", [128, D], F32)
        sq2 = s_sb[0][:, :]
        ps_main = PS(ph2, "ps2_main", [128, S], F32)
        cell["ps_tr"] = PS(ph2, "ps2_tr", [128, 1024], BF16)
        cell["ps_tr2"] = PS(ph2, "ps2_tr2", [128, 1024], BF16)
        ps_pv = PS(ph2, "ps2_pv", [128, 512], F32)
        ps_aux = PS(ph2, "ps2_aux", [128, 512], F32)
        ps_p = [ps_main[:, 0:512], ps_main[:, 512:1024]]

        bo_sb = x2t[0:1, :]
        p.emit("sync", lambda e: e.dma_start(out=bo_sb, in_=bo), inc=qS, k=16)
        p.wait("tensor", qS)
        for oc in range(4):
            p.emit("tensor", lambda e, r=bo_sb[0:1, oc * 512:(oc + 1) * 512]:
                   e.matmul(ps_aux[:, :], ones_r[0:1, :], r, start=True, stop=True), inc=tT)
            p.wait("vector", tT)
            p.emit("vector", lambda e, d=bo_b[:, oc * 512:(oc + 1) * 512]:
                   e.tensor_copy(out=d, in_=ps_aux[:, :]), inc=tV)
            p.wait("tensor", tV)

        add_n = {0: 0, 1: 0}
        exp_n = {0: 0, 1: 0}
        trans_n = {0: 0, 1: 0}
        last_add = 0
        zero_n = {0: 0, 1: 0}
        pend = []

        def flush_transposes():
            while pend:
                pp_, qt0, par0, zc = pend.pop(0)
                p.wait("tensor", tG, zc)
                for halfg in range(2):
                    pst = cell["ps_tr"] if halfg == 0 else cell["ps_tr2"]
                    for i in range(8):
                        kt = halfg * 8 + i
                        p.emit("tensor", lambda e, s=pp_[:, kt * 128:(kt + 1) * 128],
                               d=pst[:, i * 128:(i + 1) * 128]:
                               e.transpose(d, s, ident[:, :]), inc=(tT if i == 7 else None))
                    p.wait("vector", tT)
                    for i in range(8):
                        kt = halfg * 8 + i
                        p.emit("vector", lambda e, s=pst[:, i * 128:(i + 1) * 128],
                               d=pT_sb[:, kt * 512 + qt0 * 128: kt * 512 + qt0 * 128 + 128]:
                               e.tensor_copy(out=d, in_=s), inc=(tV if i == 7 else None))
                    p.wait("tensor", tV)
                trans_n[par0] = p.n(tT)

        for qh in range(2):
            for h in range(H):
                jkv = h // 2
                for qt in range(4):
                    gqt = qh * 4 + qt
                    off = 2047 - gqt * 128
                    r = (qh * H + h) * 4 + qt
                    par = r % 2
                    at_, ss_, pp_ = a_t[par], s_sb[par], p_sb[par]
                    p.wait("sync", tV, last_add)
                    p.emit("sync", lambda e, d=at_[:, 0:1024], s=T_A[h, :, off:off + 1024]:
                           e.dma_start(out=d, in_=s), inc=qS, k=16)
                    p.emit("sync", lambda e, d=at_[:, 1024:2048], s=T_X[h, :, off + 1024:off + 2048]:
                           e.dma_start(out=d, in_=s), inc=qS, k=16)
                    qtile = q_sb[:, h * NTOK + gqt * 128: h * NTOK + gqt * 128 + 128]
                    p.wait("tensor", tV, last_add)
                    p.emit("tensor", lambda e, w=qtile: e.ldweights(w))
                    for kc in range(4):
                        p.emit("tensor", lambda e, o=ps_main[:, kc * 512:(kc + 1) * 512], w=qtile,
                               r2=K_sb[:, jkv * S + kc * 512: jkv * S + (kc + 1) * 512]:
                               e.matmul(o, w, r2, start=True, stop=True), inc=(tT if kc == 3 else None))
                    qk_n = p.n(tT)
                    flush_transposes()
                    p.wait("vector", tT, qk_n)
                    p.wait("vector", qS)
                    p.wait("vector", tA, exp_n[par])
                    p.emit("vector", lambda e, d=ss_[:, :], a=at_[:, :]:
                           e.tensor_add(d, ps_main[:, :], a), inc=tV)
                    add_n[par] = p.n(tV)
                    last_add = add_n[par]
                    p.wait("vector", tA, exp_n[par])
                    p.emit("vector", lambda e, s=ss_[:, :], d=nmax[:, par * 4:par * 4 + 1]:
                           e.reduce_max(d, s, axis=AX.X, negate=True), inc=tV)
                    p.wait("vector", tV)
                    p.emit("vector", lambda e, rm=rmask[:, gqt:gqt + 1], d=ebias[:, par * 4:par * 4 + 1],
                           s=nmax[:, par * 4:par * 4 + 1]: e.tensor_mul(d, s, rm), inc=tV)
                    p.wait("scalar", tV)
                    p.wait("scalar", tT, trans_n[par])
                    p.emit("scalar", lambda e, d=pp_[:, :], s=ss_[:, :], sc=rmask[:, gqt:gqt + 1],
                           b=ebias[:, par * 4:par * 4 + 1]:
                           e.activation(d, s, AF.Exp, bias=b, scale=sc), inc=tA)
                    exp_n[par] = p.n(tA)
                    p.wait("gpsimd", tA)
                    p.emit("gpsimd", lambda e, d=pp_, kk=kcol[:, gqt:gqt + 1]:
                           e.tensor_scalar(d[:, 0:64], d[:, 0:64], kk, None, ALU.mult), inc=tG)
                    p.emit("gpsimd", lambda e, d=pp_: e.tensor_scalar(d[:, 1024:1088], d[:, 1024:1088],
                                                                     kcol2[:, 0:1], None, ALU.mult), inc=tG)
                    zero_n[par] = p.n(tG)
                    pend.append((pp_, qt, par, zero_n[par]))
                flush_transposes()
                p.emit("tensor", lambda e: e.ldweights(ones_c[:, :]))
                for kt in range(16):
                    p.emit("tensor", lambda e, r2=pT_sb[:, kt * 512:(kt + 1) * 512], s=(kt == 0), t=(kt == 15):
                           e.matmul(ps_aux[0:1, :], ones_c[:, :], r2, start=s, stop=t),
                           inc=(tT if kt == 15 else None))
                p.wait("vector", tT)
                p.emit("vector", lambda e: e.tensor_copy(out=rsum[:, :], in_=ps_aux[0:1, :]), inc=tV)
                p.wait("vector", tV)
                p.emit("vector", lambda e: e.reciprocal(rsum[:, :], rsum[:, :]), inc=tV)
                p.wait("tensor", tV)
                p.emit("tensor", lambda e: e.matmul(ps_aux[:, :], ones_r[0:1, :], rsum[0:1, :],
                                                    start=True, stop=True), inc=tT)
                p.wait("vector", tT)
                p.emit("vector", lambda e: e.tensor_copy(out=rinvb[:, :], in_=ps_aux[:, :]), inc=tV)
                p.wait("tensor", tV)
                p.wait("vector", tV)
                for kt in range(16):
                    vt = V_sb[:, kt * 1024 + jkv * 128: kt * 1024 + jkv * 128 + 128]
                    p.emit("tensor", lambda e, w=vt: e.ldweights(w))
                    p.emit("tensor", lambda e, w=vt, r2=pT_sb[:, kt * 512:(kt + 1) * 512],
                           s=(kt == 0), t=(kt == 15):
                           e.matmul(ps_pv[:, :], w, r2, start=s, stop=t), inc=(tT if kt == 15 else None))
                p.wait("vector", tT)
                p.emit("vector", lambda e, d=attnF[:, h * 512:(h + 1) * 512]:
                       e.tensor_mul(d, ps_pv[:, :], rinvb[:, :]), inc=tV)
                p.wait("tensor", tV)

            for pair in range(2):
                for oc in range(16):
                    wtl = load_wcols(Wo, oc * 128, 128)
                    wt_ready()
                    for t2 in range(2):
                        tt = pair * 2 + t2
                        proj_16(lambda dc, tt=tt: attnF[:, dc * 512 + tt * 128: dc * 512 + tt * 128 + 128],
                                lambda dc, wt=wtl: wt[:, dc * 128:(dc + 1) * 128], ps_p[t2][:, 0:128])
                        p.wait("vector", tT)
                        p.emit("vector", lambda e, s=ps_p[t2][:, 0:128],
                               d=aout[:, t2 * D + oc * 128: t2 * D + (oc + 1) * 128],
                               b=bo_b[:, oc * 128:(oc + 1) * 128]: e.tensor_add(d, s, b), inc=tV)
                        p.wait("tensor", tV)
                    wt_used()
                for t2 in range(2):
                    tt = pair * 2 + t2
                    row0 = qh * 512 + tt * 128
                    av = aout[:, t2 * D:(t2 + 1) * D]
                    p.wait("scalar", qS)
                    p.emit("scalar", lambda e, d=x2t[:, :], s=x_own[row0:row0 + 128, :]:
                           e.dma_start(out=d, in_=s), inc=qA, k=16)

                    def fin2(av=av, row0=row0):
                        p.emit("vector", lambda e, a=av: e.tensor_scalar(a, a, stat[:, 8:9], stat[:, 20:21],
                                                                         ALU.subtract, ALU.mult), inc=tV)
                        p.wait("vector", tV)
                        p.wait("vector", qA)
                        p.emit("vector", lambda e, a=av: e.tensor_add(x2t[:, :], x2t[:, :], a), inc=tV)
                        p.wait("sync", tV)
                        p.emit("sync", lambda e, d=x2d[row0:row0 + 128, :], s=x2t[:, :]:
                               e.dma_start(out=d, in_=s), inc=qS, k=16)
                    ln_stats_norm(av, sq2, fin2)

        if _DEBUG:
            dstg = aout[:, 0:4096]
            for nm_src, nm_dst, ncols in ((K_sb, dbg_k, KVH * S), (q_sb, dbg_q, H * NTOK),
                                          (V_sb, dbg_v, 16 * 1024), (attnF, dbg_af, H * 512)):
                for c0 in range(0, ncols, 4096):
                    w = min(4096, ncols - c0)
                    p.wait("vector", qS)
                    p.emit("vector", lambda e, d=dstg[:, 0:w], s=nm_src[:, c0:c0 + w]:
                           e.tensor_copy(out=d, in_=s), inc=tV)
                    p.wait("sync", tV)
                    p.emit("sync", lambda e, d=nm_dst[:, c0:c0 + w], s=dstg[:, 0:w]:
                           e.dma_start(out=d, in_=s), inc=qS, k=16)
            p.emit("sync", lambda e: e.dma_start(out=dbg_x2[:, :], in_=x2d[:, :]), inc=qS, k=16)
            p.wait("sync", qS)
        phase_barrier()
        ph2.close()
        attn_st.close()

        ph3 = contextlib.ExitStack()
        xn2 = SB(ph3, "xn2", [128, D], BF16)
        xn2F = SB(ph3, "xn2F", [128, 16 * 512], BF16)
        gated = SB(ph3, "gated", [128, 64 * 512], BF16)
        wgTp = [SB(ph3, "wgTa", [128, 16 * 512], BF16), SB(ph3, "wgTb", [128, 16 * 512], BF16)]
        wuTp = [SB(ph3, "wuTa", [128, 16 * 512], BF16), SB(ph3, "wuTb", [128, 16 * 512], BF16)]
        wdT = [SB(ph3, "wdT0", [128, 512], BF16), SB(ph3, "wdT1", [128, 512], BF16)]
        sg = SB(ph3, "sg", [128, 512], BF16)
        ffo = SB(ph3, "ffo", [128, 4 * D], F32)
        x3t = SB(ph3, "x3t", [128, D], F32)
        sq3 = SB(ph3, "sq3", [128, D], F32)
        ps_big = PS(ph3, "ps3_big", [128, S], F32)
        cell["ps_tr"] = PS(ph3, "ps3_tr", [128, 1024], BF16)
        cell["ps_tr2"] = PS(ph3, "ps3_tr2", [128, 1024], BF16)
        ps_g = PS(ph3, "ps3_g", [128, 512], F32)
        ps_u = PS(ph3, "ps3_u", [128, 512], F32)

        p.wait("sync", qT, stash_done)
        x2_stores = p.n(qS)
        p.wait("scalar", qS, x2_stores)
        for th in range(2):
            for tt in range(4):
                row0 = th * 512 + tt * 128
                p.wait("scalar", tV, ln_state["last_norm"].get("x3", 0))
                p.wait("scalar", qO)
                p.emit("scalar", lambda e, d=x3t[:, :], s=x2d[row0:row0 + 128, :]:
                       e.dma_start(out=d, in_=s), inc=qA, k=16)
                p.wait("vector", qA)
                p.wait("scalar", qA)

                def norm2():
                    p.emit("vector", lambda e: e.tensor_scalar(xn2[:, :], x3t[:, :], stat[:, 8:9],
                                                               stat[:, 20:21], ALU.subtract, ALU.mult), inc=tV)
                    ln_state["last_norm"]["x3"] = p.n(tV)
                ln_stats_norm(x3t[:, :], sq3[:, :], norm2)
                transpose_128(lambda i: xn2[:, i * 128:(i + 1) * 128], 16,
                              lambda i, tt=tt: xn2F[:, i * 512 + tt * 128: i * 512 + tt * 128 + 128], ident)

            gu_stamp = {0: 0, 1: 0}
            gu_ready = {0: 0, 1: 0}
            for hc in range(16):
                r0 = hc * 8192
                par = hc % 2
                wgT, wuT = wgTp[par], wuTp[par]
                p.wait("sync", tT, gu_stamp[par])
                p.emit("sync", lambda e, d=wgT[:, :], s=wgs[r0:r0 + 8192, :]:
                       e.dma_start(out=d, in_=s, transpose=True), inc=qS, k=16)
                p.emit("sync", lambda e, d=wuT[:, :], s=wus[r0:r0 + 8192, :]:
                       e.dma_start(out=d, in_=s, transpose=True), inc=qS, k=16)
                gu_ready[par] = p.n(qS)
                p.wait("tensor", qS, gu_ready[par])
                if _DEBUG and th == 0 and hc == 0:
                    for cc2 in range(4):
                        p.wait("vector", qS)
                        p.emit("vector", lambda e, cc2=cc2: e.tensor_copy(out=sq3[:, 0:2048],
                                                                          in_=wgT[:, cc2 * 2048:(cc2 + 1) * 2048]), inc=tV)
                        p.wait("sync", tV)
                        p.emit("sync", lambda e, cc2=cc2: e.dma_start(out=dbg_wgT[:, cc2 * 2048:(cc2 + 1) * 2048],
                                                                      in_=sq3[:, 0:2048]), inc=qS, k=16)
                    p.wait("vector", qS)
                    for cc2 in range(4):
                        p.emit("vector", lambda e, cc2=cc2: e.tensor_copy(out=sq3[:, 0:2048],
                                                                          in_=xn2F[:, cc2 * 2048:(cc2 + 1) * 2048]), inc=tV)
                        p.wait("sync", tV)
                        p.emit("sync", lambda e, cc2=cc2: e.dma_start(out=dbg_xnf2[:, cc2 * 2048:(cc2 + 1) * 2048],
                                                                      in_=sq3[:, 0:2048]), inc=qS, k=16)
                        p.wait("vector", qS)
                for hl in range(4):
                    proj_16(lambda dc, hl=hl, wgT=wgT: wgT[:, dc * 512 + hl * 128: dc * 512 + hl * 128 + 128],
                            lambda dc: xn2F[:, dc * 512:(dc + 1) * 512], ps_g, inc_last=False)
                    proj_16(lambda dc, hl=hl, wuT=wuT: wuT[:, dc * 512 + hl * 128: dc * 512 + hl * 128 + 128],
                            lambda dc: xn2F[:, dc * 512:(dc + 1) * 512], ps_u)
                    p.wait("scalar", tT)
                    p.wait("scalar", tV)
                    p.emit("scalar", lambda e: e.activation(sg[:, :], ps_g[:, :], AF.Silu), inc=tA)
                    p.wait("vector", tA)
                    hch = hc * 4 + hl
                    p.emit("vector", lambda e, d=gated[:, hch * 512:(hch + 1) * 512]:
                           e.tensor_mul(d, sg[:, :], ps_u[:, :]), inc=tV)
                    p.wait("tensor", tV)
                gu_stamp[par] = p.n(tT)
                if _DEBUG and th == 0 and hc == 1:
                    for cc2 in range(2):
                        p.emit("vector", lambda e, cc2=cc2: e.tensor_copy(out=sq3[:, 0:2048],
                                                                          in_=gated[:, cc2 * 2048:(cc2 + 1) * 2048]), inc=tV)
                        p.wait("sync", tV)
                        p.emit("sync", lambda e, cc2=cc2: e.dma_start(out=dbg_gated[:, cc2 * 2048:(cc2 + 1) * 2048],
                                                                      in_=sq3[:, 0:2048]), inc=qS, k=16)
                        p.wait("vector", qS)

            dn_n = {0: 0, 1: 0}
            dn_ready = {0: 0, 1: 0}
            for dh in range(2):
                for dq in range(2):
                    c0 = dh * 1024 + dq * 512

                    for hch in range(64):
                        wpar = hch % 2
                        wdt_ = wdT[wpar]
                        p.wait("sync", tT, dn_n[wpar])
                        p.emit("sync", lambda e, d=wdt_[:, :],
                               s=wdT_d[hch * 128:(hch + 1) * 128, c0:c0 + 512]:
                               e.dma_start(out=d, in_=s), inc=qS, k=16)
                        dn_ready[wpar] = p.n(qS)
                        p.wait("tensor", qS, dn_ready[wpar])
                        for tt in range(4):
                            lw = gated[:, hch * 512 + tt * 128: hch * 512 + tt * 128 + 128]
                            p.emit("tensor", lambda e, w=lw: e.ldweights(w))
                            p.emit("tensor", lambda e, o=ps_big[:, tt * 512:(tt + 1) * 512], w=lw,
                                   r2=wdt_[:, :], s=(hch == 0), t=(hch == 63):
                                   e.matmul(o, w, r2, start=s, stop=t), inc=(tT if tt == 3 else None))
                        dn_n[wpar] = p.n(tT)
                    p.wait("vector", tT)
                    p.wait("vector", qO)
                    for tt in range(4):
                        p.emit("vector", lambda e, s=ps_big[:, tt * 512:(tt + 1) * 512],
                               d=ffo[:, tt * D + c0: tt * D + c0 + 512]:
                               e.tensor_copy(out=d, in_=s), inc=(tV if tt == 3 else None))
                    p.wait("tensor", tV)

            if _DEBUG and th == 0:
                for cc2 in range(4):
                    p.wait("sync", tV)
                    p.emit("sync", lambda e, cc2=cc2: e.dma_start(out=dbg_ffo[:, cc2 * D:(cc2 + 1) * D],
                                                                  in_=ffo[:, cc2 * D:(cc2 + 1) * D]), inc=qS, k=16)
                p.wait("scalar", qS)
                p.wait("vector", qS)
            for tt in range(4):
                row0 = th * 512 + tt * 128
                fv = ffo[:, tt * D:(tt + 1) * D]
                p.wait("scalar", qO)
                p.emit("scalar", lambda e, d=x3t[:, :], s=x2d[row0:row0 + 128, :]:
                       e.dma_start(out=d, in_=s), inc=qA, k=16)

                def fin3(fv=fv, row0=row0):
                    p.emit("vector", lambda e, a=fv: e.tensor_scalar(a, a, stat[:, 8:9], stat[:, 20:21],
                                                                     ALU.subtract, ALU.mult), inc=tV)
                    p.wait("vector", tV)
                    p.wait("vector", qA)
                    p.wait("vector", qO)
                    p.emit("vector", lambda e, a=fv: e.tensor_add(sq3[:, :], x3t[:, :], a), inc=tV)
                    p.wait("sync", tV)
                    p.emit("sync", lambda e, d=y[row0:row0 + 128, :], s=sq3[:, :]:
                           e.dma_start(out=d, in_=s), inc=qO, k=16)
                ln_stats_norm(fv, sq3[:, :], fin3)

        ph3.close()
        p.wait("sync", qO, p.n(qO))

        p.realize(root.enter_context(nc.Block()))
    return nc


def _shard_inputs(inputs):
    t = _host_tables()
    x = np.ascontiguousarray(np.asarray(inputs["L0_attn_x"], dtype=np.float32))
    per_core = []
    shared = {
        "Wq": np.ascontiguousarray(np.asarray(inputs["L0_attn_Wq"], np.float32)),
        "Wkv": np.ascontiguousarray(np.asarray(inputs["L0_attn_Wkv"], np.float32)),
        "Wo": np.ascontiguousarray(np.asarray(inputs["L0_attn_Wo"], np.float32)),
        "bo": np.ascontiguousarray(np.asarray(inputs["L0_attn_bo"], np.float32).reshape(1, D)),
        "wg": np.ascontiguousarray(np.asarray(inputs["L0_ffn_w_gate"], np.float32)),
        "wu": np.ascontiguousarray(np.asarray(inputs["L0_ffn_w_up"], np.float32)),
        "wd": np.ascontiguousarray(np.asarray(inputs["L0_ffn_w_down"], np.float32)),
        "T_A": t["T_A"],
        "identin": t["ident16"], "ident32in": t["ident32"],
        "onesrow": t["ones_row"], "onescol": t["ones_col16"], "sq128": t["sqrt128"],
    }
    for c in range(8):
        b, hh = c // 2, c % 2
        roll = np.concatenate([np.arange(hh * NTOK, (hh + 1) * NTOK),
                               np.arange((1 - hh) * NTOK, (2 - hh) * NTOK)])
        em = np.ones((128, 8), np.float32)
        kc = np.ones((128, 8), np.float32)
        kc2 = np.ones((128, 1), np.float32)
        if hh == 0:
            em[0:64, 0] = 0.0
            kc[:, :] = 0.0
            kc[0:64, 0] = 1.0
            kc2[:, :] = 1.0
        else:
            kc[:, :] = 1.0
            kc2[:, :] = 0.0
        m = dict(shared)
        m.update({
            "x_own": np.ascontiguousarray(x[b, hh * NTOK:(hh + 1) * NTOK, :]),
            "x_oth": np.ascontiguousarray(x[b, (1 - hh) * NTOK:(2 - hh) * NTOK, :]),
            "T_X": (t["T_A"] if hh == 0 else t["T_B"]),
            "sinq_hi": np.ascontiguousarray(t["sin_hi"][:, hh * NTOK:(hh + 1) * NTOK]),
            "sinq_lo": np.ascontiguousarray(t["sin_lo"][:, hh * NTOK:(hh + 1) * NTOK]),
            "cosq_hi": np.ascontiguousarray(t["cos_hi"][:, hh * NTOK:(hh + 1) * NTOK]),
            "cosq_lo": np.ascontiguousarray(t["cos_lo"][:, hh * NTOK:(hh + 1) * NTOK]),
            "sink_hi": np.ascontiguousarray(t["sin_hi"][:, roll]),
            "sink_lo": np.ascontiguousarray(t["sin_lo"][:, roll]),
            "cosk_hi": np.ascontiguousarray(t["cos_hi"][:, roll]),
            "cosk_lo": np.ascontiguousarray(t["cos_lo"][:, roll]),
            "expmask": em, "keepcol": kc, "keepcol2": kc2,
        })
        per_core.append(m)
    return per_core


def kernel(**inputs):
    per_core = _shard_inputs(inputs)
    nc = build_program()
    res = run_bass_kernel_spmd(nc, per_core, core_ids=list(range(8)))
    out = np.empty((B, S, D), np.float32)
    for c in range(8):
        b, hh = c // 2, c % 2
        out[b, hh * NTOK:(hh + 1) * NTOK, :] = res.results[c]["y"]
    return out
